# version 02
# baseline (speedup 1.0000x reference)
import numpy as np
from contextlib import ExitStack

import concourse.bass as bass
import concourse.mybir as mybir
from concourse.bass_utils import run_bass_kernel_spmd

F32 = mybir.dt.float32
BF16 = mybir.dt.bfloat16
AF = mybir.ActivationFunctionType
ALU = mybir.AluOpType

P = 128
T = 4096
EH = 8
H = 1024
NST = 16
J = 1056
FH = 4096
CH = 512
NCHUNK = T // CH
TS = 64
NSUB = CH // TS
TT = 512
NTILE = T // TT
G = 128
EPS = 1e-5


class Sems:

    def __init__(self, nc, stack, names):
        self.h = {n: stack.enter_context(nc.semaphore(n)) for n in names}
        self.cnt = {n: 0 for n in names}
        self.waited = {}

    def inc(self, name, instr, k=1):
        instr.then_inc(self.h[name], k)
        self.cnt[name] += k
        return self.cnt[name]

    def val(self, name):
        return self.cnt[name]

    def wait(self, eng, ename, name, target):
        if target is None or target <= 0:
            return
        key = (ename, name)
        if self.waited.get(key, 0) >= target:
            return
        eng.wait_ge(self.h[name], target)
        self.waited[key] = target


def build_program(phases=4):
    nc = bass.Bass("TRN2", target_bir_lowering=False, debug=False)

    xin = nc.dram_tensor("xin", [H, T], F32, kind="ExternalInput").ap()
    wxt = nc.dram_tensor("wxt", [H, J], F32, kind="ExternalInput").ap()
    w1 = nc.dram_tensor("w1", [H, FH], F32, kind="ExternalInput").ap()
    w2 = nc.dram_tensor("w2", [FH, H], F32, kind="ExternalInput").ap()
    wot = nc.dram_tensor("wot", [H, H], F32, kind="ExternalInput").ap()
    alog = nc.dram_tensor("alog", [1, NST], F32, kind="ExternalInput").ap()
    dvec = nc.dram_tensor("dvec", [P, EH], F32, kind="ExternalInput").ap()
    b1m = nc.dram_tensor("b1m", [P, 32], F32, kind="ExternalInput").ap()
    b2m = nc.dram_tensor("b2m", [P, EH], F32, kind="ExternalInput").ap()
    yout = nc.dram_tensor("yout", [H, T], F32, kind="ExternalOutput").ap()
    a_d = nc.dram_tensor("a_d", [1, NST], F32, kind="Internal").ap()
    ab_d = nc.dram_tensor("ab_d", [2, T], F32, kind="Internal").ap()
    bc_d = nc.dram_tensor("bc_d", [32, T], F32, kind="Internal").ap()
    xn_d = nc.dram_tensor("xn_d", [H, T], BF16, kind="Internal").ap()
    attn_d = nc.dram_tensor("attn_d", [H, T], F32, kind="Internal").ap()
    acca_d = nc.dram_tensor("acca_d", [H, T], F32, kind="Internal").ap()

    xin3 = xin.rearrange("(a p) t -> p a t", p=P)
    xn3 = xn_d.rearrange("(a p) t -> p a t", p=P)
    attn3 = attn_d.rearrange("(a p) t -> p a t", p=P)
    acca3 = acca_d.rearrange("(a p) t -> p a t", p=P)
    yout3 = yout.rearrange("(a p) t -> p a t", p=P)

    with ExitStack() as outer:
        sem = Sems(nc, outer, [
            "lds",
            "sts",
            "ldg",
            "mm",
            "ev",
            "vv",
            "bs", "bg", "bt", "bc_", "bv",
        ])

        def sb(stack, name, shape, dt):
            return stack.enter_context(nc.sbuf_tensor(name, shape, dt))

        A_bc = sb(outer, "A_bc", [P, NST], F32)
        D_sb = sb(outer, "D_sb", [P, EH], F32)
        B1_sb = sb(outer, "B1_sb", [P, 32], F32)
        B2_sb = sb(outer, "B2_sb", [P, EH], F32)
        ones = sb(outer, "ones", [P, 1], F32)
        arow = sb(outer, "arow", [1, NST], F32)

        ps = [outer.enter_context(nc.psum_tensor(f"ps{i}", [P, 512], F32))
              for i in range(6)]
        bank_free = {i: None for i in range(6)}

        block = outer.enter_context(nc.Block())
        barrier_n = [0]

        def barrier():
            barrier_n[0] += 1
            k = barrier_n[0]
            pairs = [("sync", "bs"), ("gpsimd", "bg"), ("tensor", "bt"),
                     ("scalar", "bc_"), ("vector", "bv")]
            for ename, s in pairs:
                @getattr(block, ename)
                def _(eng, s=s):
                    eng.sem_inc(sem.h[s], 1)
                sem.cnt[s] += 1
            for ename, _s in pairs:
                @getattr(block, ename)
                def _(eng, ename=ename):
                    for _, s2 in pairs:
                        sem.wait(eng, ename, s2, k)

        with ExitStack() as s13:
            WxT = sb(s13, "WxT", [P, EH * J], BF16)
            Wot = sb(s13, "Wot", [P, EH * H], BF16)

            @block.sync
            def _(sync):
                sync.dma_start(out=arow[:, :], in_=alog[:, :]).then_inc(sem.h["lds"], 16)
                sem.cnt["lds"] += 16
                sync.dma_start(out=D_sb[:, :], in_=dvec[:, :]).then_inc(sem.h["lds"], 16)
                sem.cnt["lds"] += 16
                sync.dma_start(out=B1_sb[:, :], in_=b1m[:, :]).then_inc(sem.h["lds"], 16)
                sem.cnt["lds"] += 16
                sync.dma_start(out=B2_sb[:, :], in_=b2m[:, :]).then_inc(sem.h["lds"], 16)
                sem.cnt["lds"] += 16
            ph0_lds = sem.val("lds")

            @block.gpsimd
            def _(gp):
                i = gp.dma_start(out=WxT[:, :].rearrange("p (a j) -> p a j", a=EH),
                                 in_=wxt.rearrange("(a p) j -> p a j", p=P))
                sem.inc("ldg", i, 16)
                i = gp.dma_start(out=Wot[:, :].rearrange("p (a e) -> p a e", a=EH),
                                 in_=wot.rearrange("(a p) e -> p a e", p=P))
                sem.inc("ldg", i, 16)
            wx_ld = sem.val("ldg")

            @block.scalar
            def _(sc):
                sem.wait(sc, "scalar", "lds", ph0_lds)
                i = sc.activation(out=arow[:, :], in_=arow[:, :], func=AF.Exp)
                sem.inc("ev", i)
            a_exp = sem.val("ev")

            @block.vector
            def _(v):
                v.memset(ones[:, :], 1.0)
                sem.wait(v, "vector", "ev", a_exp)
                i = v.tensor_scalar(out=arow[:, :], in0=arow[:, :], scalar1=-1.0,
                                    scalar2=None, op0=ALU.mult)
                sem.inc("vv", i)
            a_neg = sem.val("vv")

            @block.sync
            def _(sync):
                sem.wait(sync, "sync", "vv", a_neg)
                i = sync.dma_start(out=a_d[:, :], in_=arow[:, :])
                sem.inc("sts", i, 16)
            a_spill = sem.val("sts")

            @block.gpsimd
            def _(gp):
                sem.wait(gp, "gpsimd", "sts", a_spill)
                i = gp.dma_start(out=A_bc[:, :], in_=a_d[0:1, :].broadcast_to([P, NST]))
                sem.inc("ldg", i, 16)
            abc_ld = sem.val("ldg")

            with ExitStack() as s1:
                xt = [sb(s1, f"xt{b}", [P, EH * TT], F32) for b in range(2)]
                xsq = sb(s1, "xsq", [P, EH * TT], F32)
                xn_t = [sb(s1, f"xnt{b}", [P, EH * TT], BF16) for b in range(2)]
                a_bc = sb(s1, "a_bc", [P, TT], F32)
                b_bc = sb(s1, "b_bc", [P, TT], F32)
                rows = sb(s1, "rows", [1, 512 * 10], F32)

                def row(i):
                    return rows[:, i * 512:(i + 1) * 512]

                x_loaded = []
                xt_free = [None, None]
                xn_free = [None, None]
                xsq_free = None
                abc_free = None
                ab_sp_prev = None
                spill_cnt = []

                for tt in range(NTILE):
                    b = tt % 2
                    t0 = tt * TT

                    @block.sync
                    def _(sync, b=b, t0=t0):
                        if xt_free[b] is not None:
                            sem.wait(sync, "sync", *xt_free[b])
                        i = sync.dma_start(out=xt[b][:, :].rearrange("p (a t) -> p a t", a=EH),
                                           in_=xin3[:, :, t0:t0 + TT])
                        sem.inc("lds", i, 16)
                    x_loaded.append(sem.val("lds"))

                    @block.scalar
                    def _(sc, b=b):
                        sem.wait(sc, "scalar", "lds", x_loaded[-1])
                        if xsq_free is not None:
                            sem.wait(sc, "scalar", *xsq_free)
                        i = sc.activation(out=xsq[:, :], in_=xt[b][:, :], func=AF.Square)
                        sem.inc("ev", i)
                    xsq_done = sem.val("ev")

                    @block.tensor
                    def _(te, b=b):
                        sem.wait(te, "tensor", "lds", x_loaded[-1])
                        sem.wait(te, "tensor", "ev", xsq_done)
                        if bank_free[4] is not None:
                            sem.wait(te, "tensor", *bank_free[4])
                        if bank_free[5] is not None:
                            sem.wait(te, "tensor", *bank_free[5])
                        for kc in range(EH):
                            te.matmul(out=ps[4][0:1, 0:TT], lhsT=ones[:, :],
                                      rhs=xt[b][:, kc * TT:(kc + 1) * TT],
                                      start=(kc == 0), stop=(kc == EH - 1))
                        for kc in range(EH):
                            i = te.matmul(out=ps[5][0:1, 0:TT], lhsT=ones[:, :],
                                          rhs=xsq[:, kc * TT:(kc + 1) * TT],
                                          start=(kc == 0), stop=(kc == EH - 1))
                        sem.inc("mm", i)
                    stats_done = sem.val("mm")
                    xsq_free = ("mm", stats_done)

                    @block.vector
                    def _(v):
                        sem.wait(v, "vector", "mm", stats_done)
                        v.tensor_scalar(out=row(0), in0=ps[4][0:1, 0:TT],
                                        scalar1=1.0 / H, scalar2=None, op0=ALU.mult)
                        v.tensor_scalar(out=row(1), in0=ps[5][0:1, 0:TT],
                                        scalar1=1.0 / H, scalar2=None, op0=ALU.mult)
                        v.tensor_mul(row(2), row(0), row(0))
                        v.tensor_sub(row(3), row(1), row(2))
                        i = v.tensor_scalar(out=row(4), in0=row(3), scalar1=EPS,
                                            scalar2=None, op0=ALU.add)
                        sem.inc("vv", i)
                    veps_done = sem.val("vv")
                    bank_free[4] = ("vv", veps_done)
                    bank_free[5] = ("vv", veps_done)

                    @block.scalar
                    def _(sc):
                        sem.wait(sc, "scalar", "vv", veps_done)
                        i = sc.activation(out=row(5), in_=row(4), func=AF.Sqrt)
                        sem.inc("ev", i)
                    sq_done = sem.val("ev")

                    @block.vector
                    def _(v):
                        sem.wait(v, "vector", "ev", sq_done)
                        if ab_sp_prev is not None:
                            sem.wait(v, "vector", "sts", ab_sp_prev)
                        v.reciprocal(row(6), row(5))
                        v.tensor_mul(row(2), row(6), row(6))
                        v.tensor_mul(row(1), row(2), row(4))
                        v.tensor_scalar(out=row(2), in0=row(1), scalar1=-0.5,
                                        scalar2=1.5, op0=ALU.mult, op1=ALU.add)
                        v.tensor_mul(row(7), row(6), row(2))
                        v.tensor_mul(row(8), row(0), row(7))
                        i = v.tensor_scalar(out=row(8), in0=row(8), scalar1=-1.0,
                                            scalar2=None, op0=ALU.mult)
                        sem.inc("vv", i)
                    ab_done = sem.val("vv")

                    @block.sync
                    def _(sync, t0=t0):
                        sem.wait(sync, "sync", "vv", ab_done)
                        sync.dma_start(out=ab_d[0:1, t0:t0 + TT], in_=row(7)).then_inc(sem.h["sts"], 16)
                        sem.cnt["sts"] += 16
                        i = sync.dma_start(out=ab_d[1:2, t0:t0 + TT], in_=row(8))
                        sem.inc("sts", i, 16)
                    ab_sp = sem.val("sts")
                    ab_sp_prev = ab_sp

                    @block.gpsimd
                    def _(gp, t0=t0):
                        sem.wait(gp, "gpsimd", "sts", ab_sp)
                        if abc_free is not None:
                            sem.wait(gp, "gpsimd", *abc_free)
                        gp.dma_start(out=a_bc[:, :],
                                     in_=ab_d[0:1, t0:t0 + TT].broadcast_to([P, TT])).then_inc(sem.h["ldg"], 16)
                        sem.cnt["ldg"] += 16
                        i = gp.dma_start(out=b_bc[:, :],
                                         in_=ab_d[1:2, t0:t0 + TT].broadcast_to([P, TT]))
                        sem.inc("ldg", i, 16)
                    abc_ld2 = sem.val("ldg")

                    @block.vector
                    def _(v, b=b):
                        sem.wait(v, "vector", "ldg", abc_ld2)
                        if xn_free[b] is not None:
                            sem.wait(v, "vector", *xn_free[b])
                        v.tensor_tensor(
                            out=xt[b][:, :].rearrange("p (a t) -> p a t", a=EH),
                            in0=xt[b][:, :].rearrange("p (a t) -> p a t", a=EH),
                            in1=a_bc[:, :].unsqueeze(1).broadcast_to([P, EH, TT]),
                            op=ALU.mult)
                        i = v.tensor_tensor(
                            out=xn_t[b][:, :].rearrange("p (a t) -> p a t", a=EH),
                            in0=xt[b][:, :].rearrange("p (a t) -> p a t", a=EH),
                            in1=b_bc[:, :].unsqueeze(1).broadcast_to([P, EH, TT]),
                            op=ALU.add)
                        sem.inc("vv", i)
                    xn_done = sem.val("vv")
                    xt_free[b] = ("vv", xn_done)
                    abc_free = ("vv", xn_done)

                    @block.sync
                    def _(sync, b=b, t0=t0):
                        sem.wait(sync, "sync", "vv", xn_done)
                        i = sync.dma_start(out=xn3[:, :, t0:t0 + TT],
                                           in_=xn_t[b][:, :].rearrange("p (a t) -> p a t", a=EH))
                        sem.inc("sts", i, 16)
                    spill_cnt.append(sem.val("sts"))
                    xn_free[b] = ("sts", spill_cnt[-1])

            barrier()

            if phases < 3:
                @block.sync
                def _(sync):
                    sem.wait(sync, "sync", "sts", sem.val("sts"))
                return nc
            with ExitStack() as s3:
                xn_c = [sb(s3, f"xnc{b}", [P, EH * CH], BF16) for b in range(2)]
                delta = sb(s3, "dl", [P, EH * CH], F32)
                dtx = sb(s3, "dtx", [P, EH * CH], BF16)
                dxn = sb(s3, "dxn", [P, EH * CH], BF16)
                dA = [sb(s3, f"dA{b}", [P, G * TS], BF16) for b in range(2)]
                ub = sb(s3, "ub", [P, G * TS], BF16)
                hb = sb(s3, "hb", [P, G * TS], BF16)
                l4f = sb(s3, "l4f", [P, EH * TS], F32)
                yb = [sb(s3, f"yb{b}", [P, EH * CH], BF16) for b in range(2)]
                att_t = sb(s3, "att_t", [P, EH * CH], F32)
                carry = sb(s3, "carry", [P, G], F32)
                dA0 = sb(s3, "dA0v", [P, G], F32)
                tmpg = sb(s3, "tmpg", [P, G], F32)
                BCs = sb(s3, "BCs", [32, CH], F32)
                Bb = [sb(s3, f"Bb{b}", [P, NST * TS], BF16) for b in range(2)]
                Cb = [sb(s3, f"Cb{b}", [P, NST * TS], BF16) for b in range(2)]

                @block.vector
                def _(v):
                    v.memset(carry[:, :], 0.0)

                @block.sync
                def _(sync):
                    sem.wait(sync, "sync", "sts", spill_cnt[0])
                    i = sync.dma_start(out=xn_c[0][:, :].rearrange("p (a t) -> p a t", a=EH),
                                       in_=xn3[:, :, 0:CH])
                    sem.inc("lds", i, 16)
                xn_ld = [sem.val("lds")]

                sub_done = []
                y_done = []
                wout_ev = []
                att_sp = []
                xnc_free = [None, None]
                delta_free = [None, None]
                y_free = [None, None]
                dlb_evcnt = {}

                def emit_wout(cc):
                    bcy = cc % 2
                    for eb in range(EH):
                        bank = 2 + eb % 2

                        @block.tensor
                        def _(te, cc=cc, bcy=bcy, eb=eb, bank=bank):
                            sem.wait(te, "tensor", "vv", y_done[cc])
                            sem.wait(te, "tensor", "ldg", wx_ld)
                            if bank_free[bank] is not None:
                                sem.wait(te, "tensor", *bank_free[bank])
                            for kc in range(EH):
                                i = te.matmul(
                                    out=ps[bank][:, 0:CH],
                                    lhsT=Wot[:, kc * H + eb * P:kc * H + (eb + 1) * P],
                                    rhs=yb[bcy][:, kc * CH:(kc + 1) * CH],
                                    start=(kc == 0), stop=(kc == EH - 1))
                            sem.inc("mm", i)
                            dlb_evcnt[("wo", cc, eb)] = sem.val("mm")

                        @block.scalar
                        def _(sc, cc=cc, eb=eb, bank=bank):
                            sem.wait(sc, "scalar", "mm", dlb_evcnt[("wo", cc, eb)])
                            if eb == 0 and att_sp:
                                sem.wait(sc, "scalar", "sts", att_sp[-1])
                            i = sc.activation(out=att_t[:, eb * CH:(eb + 1) * CH],
                                              in_=ps[bank][:, 0:CH], func=AF.Copy)
                            sem.inc("ev", i)
                            bank_free[bank] = ("ev", sem.val("ev"))
                    wout_ev.append(sem.val("ev"))

                    @block.sync
                    def _(sync, cc=cc):
                        sem.wait(sync, "sync", "ev", wout_ev[-1])
                        i = sync.dma_start(out=attn3[:, :, cc * CH:(cc + 1) * CH],
                                           in_=att_t[:, :].rearrange("p (a t) -> p a t", a=EH))
                        sem.inc("sts", i, 16)
                    att_sp.append(sem.val("sts"))
                    y_free[cc % 2] = ("mm", dlb_evcnt[("wo", cc, EH - 1)])

                for cc in range(NCHUNK):
                    bcc = cc % 2
                    ct0 = cc * CH

                    if cc + 1 < NCHUNK:
                        @block.sync
                        def _(sync, cc=cc):
                            nb = (cc + 1) % 2
                            sem.wait(sync, "sync", "sts", spill_cnt[cc + 1])
                            if xnc_free[nb] is not None:
                                sem.wait(sync, "sync", *xnc_free[nb])
                            i = sync.dma_start(
                                out=xn_c[nb][:, :].rearrange("p (a t) -> p a t", a=EH),
                                in_=xn3[:, :, (cc + 1) * CH:(cc + 2) * CH])
                            sem.inc("lds", i, 16)
                        xn_ld.append(sem.val("lds"))

                    for jb in [8] + list(range(8)):
                        bank = 4 if jb == 8 else jb % 2
                        mwid = 32 if jb == 8 else P

                        @block.tensor
                        def _(te, cc=cc, bcc=bcc, jb=jb, bank=bank, mwid=mwid):
                            sem.wait(te, "tensor", "lds", xn_ld[cc])
                            sem.wait(te, "tensor", "ldg", wx_ld)
                            if bank_free[bank] is not None:
                                sem.wait(te, "tensor", *bank_free[bank])
                            for kc in range(EH):
                                i = te.matmul(
                                    out=ps[bank][0:mwid, 0:CH],
                                    lhsT=WxT[:, kc * J + jb * P:kc * J + jb * P + mwid],
                                    rhs=xn_c[bcc][:, kc * CH:(kc + 1) * CH],
                                    start=(kc == 0), stop=(kc == EH - 1))
                            sem.inc("mm", i)
                            dlb_evcnt[(cc, jb)] = sem.val("mm")

                        @block.scalar
                        def _(sc, cc=cc, jb=jb, bank=bank):
                            sem.wait(sc, "scalar", "mm", dlb_evcnt[(cc, jb)])
                            if jb == 8:
                                if cc >= 1:
                                    sem.wait(sc, "scalar", "sts", dlb_evcnt[(cc - 1, "bcsp")])
                                i = sc.activation(out=BCs[:, :], in_=ps[bank][0:32, 0:CH],
                                                  func=AF.Copy)
                                sem.inc("ev", i)
                                dlb_evcnt[(cc, "bc")] = sem.val("ev")
                            else:
                                if jb == 0 and delta_free[0] is not None:
                                    sem.wait(sc, "scalar", *delta_free[0])
                                sl = delta[:, jb * CH:(jb + 1) * CH]
                                sc.activation(out=sl, in_=ps[bank][:, 0:CH], func=AF.Exp)
                                i = sc.activation(out=sl, in_=sl, func=AF.Ln, bias=1.0)
                                sem.inc("ev", i)
                            bank_free[bank] = ("ev", sem.val("ev"))
                    dlb_evcnt[(cc, "dl")] = sem.val("ev")

                    @block.sync
                    def _(sync, cc=cc, ct0=ct0):
                        sem.wait(sync, "sync", "ev", dlb_evcnt[(cc, "bc")])
                        i = sync.dma_start(out=bc_d[:, ct0:ct0 + CH], in_=BCs[:, :])
                        sem.inc("sts", i, 16)
                    dlb_evcnt[(cc, "bcsp")] = sem.val("sts")

                    @block.vector
                    def _(v, cc=cc, bcc=bcc):
                        sem.wait(v, "vector", "ev", dlb_evcnt[(cc, "dl")])
                        sem.wait(v, "vector", "lds", xn_ld[cc])
                        v.tensor_tensor(out=dtx[:, :], in0=delta[:, :],
                                        in1=xn_c[bcc][:, :], op=ALU.mult)
                        for chx in range(EH):
                            i = v.tensor_scalar(out=dxn[:, chx * CH:(chx + 1) * CH],
                                                in0=xn_c[bcc][:, chx * CH:(chx + 1) * CH],
                                                scalar1=D_sb[:, chx:chx + 1],
                                                scalar2=None, op0=ALU.mult)
                        sem.inc("vv", i)
                    dtx_done = sem.val("vv")
                    xnc_free[bcc] = ("vv", dtx_done)

                    for s in range(NSUB):
                        gs = cc * NSUB + s
                        bs_ = s % 2
                        ts0 = s * TS
                        gt0 = ct0 + ts0

                        @block.gpsimd
                        def _(gp, gs=gs, bs_=bs_, gt0=gt0, cc=cc):
                            sem.wait(gp, "gpsimd", "sts", dlb_evcnt[(cc, "bcsp")])
                            if gs >= 2:
                                sem.wait(gp, "gpsimd", "vv", sub_done[gs - 2])
                            gp.dma_start(
                                out=Bb[bs_][:, :].rearrange("p (n t) -> p n t", n=NST),
                                in_=bc_d[0:NST, gt0:gt0 + TS].unsqueeze(0).broadcast_to([P, NST, TS])
                            ).then_inc(sem.h["ldg"], 16)
                            sem.cnt["ldg"] += 16
                            i = gp.dma_start(
                                out=Cb[bs_][:, :].rearrange("p (n t) -> p n t", n=NST),
                                in_=bc_d[NST:32, gt0:gt0 + TS].unsqueeze(0).broadcast_to([P, NST, TS]))
                            sem.inc("ldg", i, 16)
                        bc_ld = sem.val("ldg")

                        @block.scalar
                        def _(sc, gs=gs, bs_=bs_, ts0=ts0, bcc=bcc):
                            if gs >= 2:
                                sem.wait(sc, "scalar", "vv", sub_done[gs - 2])
                            d3 = delta[:, :].rearrange("p (a t) -> p a t", a=EH)[:, :, ts0:ts0 + TS]
                            for n in range(NST):
                                i = sc.activation(
                                    out=dA[bs_][:, n * EH * TS:(n + 1) * EH * TS]
                                        .rearrange("p (a t) -> p a t", a=EH),
                                    in_=d3, func=AF.Exp, scale=A_bc[:, n:n + 1])
                            sem.inc("ev", i)
                        exp_done = sem.val("ev")

                        yfree = y_free[cc % 2] if s == 0 else None

                        @block.vector
                        def _(v, gs=gs, bs_=bs_, ts0=ts0, yfree=yfree):
                            sem.wait(v, "vector", "ldg", bc_ld)
                            sem.wait(v, "vector", "ev", exp_done)
                            if yfree is not None:
                                sem.wait(v, "vector", *yfree)
                            d4 = dtx[:, :].rearrange("p (a t) -> p a t", a=EH)[:, :, ts0:ts0 + TS]
                            u4 = ub[:, :].rearrange("p (n a t) -> p n a t", n=NST, a=EH)
                            v.tensor_tensor(
                                out=u4,
                                in0=d4.unsqueeze(1).broadcast_to([P, NST, EH, TS]),
                                in1=Bb[bs_][:, :].rearrange("p (n t) -> p n t", n=NST)
                                    .unsqueeze(2).broadcast_to([P, NST, EH, TS]),
                                op=ALU.mult)
                            dA3 = dA[bs_][:, :].rearrange("p (g t) -> p g t", g=G)
                            v.tensor_copy(out=dA0[:, :].unsqueeze(2), in_=dA3[:, :, 0:1])
                            v.tensor_mul(tmpg[:, :], dA0[:, :], carry[:, :])
                            u3 = ub[:, :].rearrange("p (g t) -> p g t", g=G)
                            v.tensor_tensor(out=u3[:, :, 0:1], in0=u3[:, :, 0:1],
                                            in1=tmpg[:, :].unsqueeze(2), op=ALU.add)
                            v.memset(dA3[:, :, 0:1], 0.0)
                            v.tensor_tensor_scan(out=hb[:, :], data0=dA[bs_][:, :],
                                                 data1=ub[:, :], initial=0.0,
                                                 op0=ALU.mult, op1=ALU.add)
                            h3 = hb[:, :].rearrange("p (g t) -> p g t", g=G)
                            v.tensor_copy(out=carry[:, :].unsqueeze(2), in_=h3[:, :, TS - 1:TS])
                            v.tensor_tensor(
                                out=u4, in0=hb[:, :].rearrange("p (n a t) -> p n a t", n=NST, a=EH),
                                in1=Cb[bs_][:, :].rearrange("p (n t) -> p n t", n=NST)
                                    .unsqueeze(2).broadcast_to([P, NST, EH, TS]),
                                op=ALU.mult)
                            half = G * TS // 2
                            v.tensor_tensor(out=hb[:, 0:half], in0=ub[:, 0:half],
                                            in1=ub[:, half:2 * half], op=ALU.add)
                            q = half // 2
                            v.tensor_tensor(out=hb[:, half:half + q], in0=hb[:, 0:q],
                                            in1=hb[:, q:2 * q], op=ALU.add)
                            e8 = q // 2
                            v.tensor_tensor(out=hb[:, half + q:half + q + e8],
                                            in0=hb[:, half:half + e8],
                                            in1=hb[:, half + e8:half + 2 * e8], op=ALU.add)
                            e16 = e8 // 2
                            v.tensor_tensor(out=l4f[:, :], in0=hb[:, half + q:half + q + e16],
                                            in1=hb[:, half + q + e16:half + q + 2 * e16], op=ALU.add)
                            i = v.tensor_tensor(
                                out=yb[gs // NSUB % 2][:, :].rearrange("p (a t) -> p a t", a=EH)[:, :, ts0:ts0 + TS],
                                in0=l4f[:, :].rearrange("p (a t) -> p a t", a=EH),
                                in1=dxn[:, :].rearrange("p (a t) -> p a t", a=EH)[:, :, ts0:ts0 + TS],
                                op=ALU.add)
                            sem.inc("vv", i)
                        sub_done.append(sem.val("vv"))

                    y_done.append(sub_done[-1])
                    delta_free[0] = ("vv", sub_done[-1])
                    if cc + 2 < NCHUNK:
                        pass
                    emit_wout(cc)

            barrier()

        barrier()

        if phases < 4:
            @block.sync
            def _(sync):
                sem.wait(sync, "sync", "sts", sem.val("sts"))
            return nc
        with ExitStack() as s4:
            w1h = sb(s4, "w1h", [P, EH * (FH // 2)], BF16)
            w2h = sb(s4, "w2h", [P, (FH // 256) * H], BF16)
            xn_t4 = [sb(s4, f"xn4{b}", [P, EH * TT], BF16) for b in range(2)]
            res_t = [sb(s4, f"res{b}", [P, EH * TT], F32) for b in range(2)]
            att_t4 = [sb(s4, "at4", [P, EH * TT], F32)] * 2
            h_bf = sb(s4, "h_bf", [P, 16 * TT], BF16)
            fB = sb(s4, "fB", [P, EH * TT], F32)
            out_t = [sb(s4, "out1", [P, EH * TT], F32)] * 2

            HHALF = FH // 2

            for pas in range(2):
                hoff = pas * HHALF

                @block.gpsimd
                def _(gp, pas=pas, hoff=hoff):
                    if pas == 1:
                        sem.wait(gp, "gpsimd", "mm", w_used[0])
                    gp.dma_start(
                        out=w1h[:, :].rearrange("p (a h) -> p a h", a=EH),
                        in_=w1.rearrange("(a p) h -> p a h", p=P)[:, :, hoff:hoff + HHALF]
                    ).then_inc(sem.h["ldg"], 16)
                    sem.cnt["ldg"] += 16
                    i = gp.dma_start(
                        out=w2h[:, :].rearrange("p (a e) -> p a e", a=16),
                        in_=w2.rearrange("(a p) e -> p a e", p=P)[:, pas * 16:(pas + 1) * 16, :])
                    sem.inc("ldg", i, 16)
                w_ld = sem.val("ldg")
                w_used = [None]

                gelu_cnt = {}
                ffn2_cnt = {}
                res_free = [None, None]
                xn4_free = [None, None]
                att4_free = [None, None]
                out_free = [None, None]
                h_free = None
                fin_prev = None

                for tt in range(NTILE):
                    b = tt % 2
                    t0 = tt * TT

                    @block.sync
                    def _(sync, pas=pas, b=b, t0=t0):
                        if xn4_free[b] is not None:
                            sem.wait(sync, "sync", *xn4_free[b])
                        sync.dma_start(out=xn_t4[b][:, :].rearrange("p (a t) -> p a t", a=EH),
                                       in_=xn3[:, :, t0:t0 + TT]).then_inc(sem.h["lds"], 16)
                        sem.cnt["lds"] += 16
                        if res_free[b] is not None:
                            sem.wait(sync, "sync", *res_free[b])
                        src = xin3 if pas == 0 else acca3
                        i = sync.dma_start(out=res_t[b][:, :].rearrange("p (a t) -> p a t", a=EH),
                                           in_=src[:, :, t0:t0 + TT])
                        sem.inc("lds", i, 16)
                        if pas == 1:
                            if att4_free[b] is not None:
                                sem.wait(sync, "sync", *att4_free[b])
                            i = sync.dma_start(out=att_t4[b][:, :].rearrange("p (a t) -> p a t", a=EH),
                                               in_=attn3[:, :, t0:t0 + TT])
                            sem.inc("lds", i, 16)
                    t_ld = sem.val("lds")

                    for hb_ in range(16):
                        bank = hb_ % 2

                        @block.tensor
                        def _(te, pas=pas, b=b, tt=tt, hb_=hb_, bank=bank):
                            sem.wait(te, "tensor", "lds", t_ld)
                            sem.wait(te, "tensor", "ldg", w_ld)
                            if hb_ == 0 and h_free is not None:
                                sem.wait(te, "tensor", *h_free)
                            if bank_free[bank] is not None:
                                sem.wait(te, "tensor", *bank_free[bank])
                            for kc in range(EH):
                                i = te.matmul(
                                    out=ps[bank][:, 0:TT],
                                    lhsT=w1h[:, kc * HHALF + hb_ * P:kc * HHALF + (hb_ + 1) * P],
                                    rhs=xn_t4[b][:, kc * TT:(kc + 1) * TT],
                                    start=(kc == 0), stop=(kc == EH - 1))
                            sem.inc("mm", i)
                            gelu_cnt[(pas, tt, hb_)] = sem.val("mm")

                        @block.scalar
                        def _(sc, pas=pas, tt=tt, hb_=hb_, bank=bank):
                            sem.wait(sc, "scalar", "mm", gelu_cnt[(pas, tt, hb_)])
                            i = sc.activation(out=h_bf[:, hb_ * TT:(hb_ + 1) * TT],
                                              in_=ps[bank][:, 0:TT], func=AF.Gelu,
                                              bias=B1_sb[:, pas * 16 + hb_:pas * 16 + hb_ + 1])
                            sem.inc("ev", i)
                            bank_free[bank] = ("ev", sem.val("ev"))
                    gelu_done = sem.val("ev")

                    for eb in range(EH):
                        bank = 2 + eb % 2

                        @block.tensor
                        def _(te, pas=pas, tt=tt, eb=eb, bank=bank):
                            sem.wait(te, "tensor", "ev", gelu_done)
                            if bank_free[bank] is not None:
                                sem.wait(te, "tensor", *bank_free[bank])
                            for kc in range(16):
                                i = te.matmul(
                                    out=ps[bank][:, 0:TT],
                                    lhsT=w2h[:, kc * H + eb * P:kc * H + (eb + 1) * P],
                                    rhs=h_bf[:, kc * TT:(kc + 1) * TT],
                                    start=(kc == 0), stop=(kc == 15))
                            sem.inc("mm", i)
                            ffn2_cnt[(pas, tt, eb)] = sem.val("mm")

                        if pas == 0:
                            @block.vector
                            def _(v, tt=tt, b=b, eb=eb, bank=bank):
                                sem.wait(v, "vector", "mm", ffn2_cnt[(0, tt, eb)])
                                if eb == 0 and out_free[b] is not None:
                                    sem.wait(v, "vector", *out_free[b])
                                i = v.tensor_tensor(out=out_t[b][:, eb * TT:(eb + 1) * TT],
                                                    in0=ps[bank][:, 0:TT],
                                                    in1=res_t[b][:, eb * TT:(eb + 1) * TT],
                                                    op=ALU.add)
                                sem.inc("vv", i)
                                bank_free[bank] = ("vv", sem.val("vv"))
                        else:
                            @block.scalar
                            def _(sc, tt=tt, eb=eb, bank=bank):
                                if eb == 0 and fin_prev is not None:
                                    sem.wait(sc, "scalar", "vv", fin_prev)
                                sem.wait(sc, "scalar", "mm", ffn2_cnt[(1, tt, eb)])
                                i = sc.activation(out=fB[:, eb * TT:(eb + 1) * TT],
                                                  in_=ps[bank][:, 0:TT], func=AF.Identity,
                                                  bias=B2_sb[:, eb:eb + 1])
                                sem.inc("ev", i)
                                bank_free[bank] = ("ev", sem.val("ev"))

                    h_free = ("mm", ffn2_cnt[(pas, tt, EH - 1)])
                    if tt == NTILE - 1:
                        w_used[0] = ffn2_cnt[(pas, tt, EH - 1)]

                    if pas == 0:
                        acc_done = sem.val("vv")
                        res_free[b] = ("vv", acc_done)
                        xn4_free[b] = h_free

                        @block.sync
                        def _(sync, tt=tt, b=b, t0=t0):
                            sem.wait(sync, "sync", "vv", acc_done)
                            i = sync.dma_start(out=acca3[:, :, t0:t0 + TT],
                                               in_=out_t[b][:, :].rearrange("p (a t) -> p a t", a=EH))
                            sem.inc("sts", i, 16)
                        out_free[b] = ("sts", sem.val("sts"))
                    else:
                        fB_done = sem.val("ev")

                        @block.vector
                        def _(v, tt=tt, b=b):
                            sem.wait(v, "vector", "ev", fB_done)
                            if out_free[b] is not None:
                                sem.wait(v, "vector", *out_free[b])
                            v.tensor_tensor(out=out_t[b][:, :], in0=fB[:, :],
                                            in1=res_t[b][:, :], op=ALU.add)
                            i = v.tensor_tensor(out=out_t[b][:, :], in0=out_t[b][:, :],
                                                in1=att_t4[b][:, :], op=ALU.add)
                            sem.inc("vv", i)
                        fin_done = sem.val("vv")
                        fin_prev = fin_done
                        res_free[b] = ("vv", fin_done)
                        att4_free[b] = ("vv", fin_done)
                        xn4_free[b] = h_free

                        @block.sync
                        def _(sync, tt=tt, b=b, t0=t0):
                            sem.wait(sync, "sync", "vv", fin_done)
                            i = sync.dma_start(out=yout3[:, :, t0:t0 + TT],
                                               in_=out_t[b][:, :].rearrange("p (a t) -> p a t", a=EH))
                            sem.inc("sts", i, 16)
                        out_free[b] = ("sts", sem.val("sts"))

            @block.sync
            def _(sync):
                sem.wait(sync, "sync", "sts", sem.val("sts"))

    return nc


def kernel(L0_attn_x, L0_attn_Wx, L0_attn_A_log, L0_attn_D, L0_attn_Wout,
           L0_ffn_w1, L0_ffn_b1, L0_ffn_w2, L0_ffn_b2):
    x = np.asarray(L0_attn_x, dtype=np.float32)
    Wx = np.asarray(L0_attn_Wx, dtype=np.float32)
    A_log = np.asarray(L0_attn_A_log, dtype=np.float32)
    D = np.asarray(L0_attn_D, dtype=np.float32)
    Wout = np.asarray(L0_attn_Wout, dtype=np.float32)
    w1 = np.asarray(L0_ffn_w1, dtype=np.float32)
    b1 = np.asarray(L0_ffn_b1, dtype=np.float32)
    w2 = np.asarray(L0_ffn_w2, dtype=np.float32)
    b2 = np.asarray(L0_ffn_b2, dtype=np.float32)

    wxt = np.ascontiguousarray(np.concatenate([Wx[32:], Wx[:32]], axis=0).T)
    wot = np.ascontiguousarray(Wout.T)
    w1c = np.ascontiguousarray(w1)
    w2c = np.ascontiguousarray(w2)
    alog = np.ascontiguousarray(A_log.reshape(1, 16))
    dvec = np.ascontiguousarray(D.reshape(8, 128).T)
    b1m = np.ascontiguousarray(b1.reshape(32, 128).T)
    b2m = np.ascontiguousarray(b2.reshape(8, 128).T)

    nc = build_program()
    in_maps = []
    for i in range(8):
        in_maps.append({
            "xin": np.ascontiguousarray(x[i].T),
            "wxt": wxt, "w1": w1c, "w2": w2c, "wot": wot,
            "alog": alog, "dvec": dvec, "b1m": b1m, "b2m": b2m,
        })
    res = run_bass_kernel_spmd(nc, in_maps, core_ids=list(range(8)))
    out = np.stack([np.asarray(r["yout"], dtype=np.float32).T for r in res.results], axis=0)
    return np.ascontiguousarray(out)


# version 03
# speedup vs baseline: 1.0215x; 1.0215x over previous
import numpy as np
from contextlib import ExitStack

import concourse.bass as bass
import concourse.mybir as mybir
from concourse.bass_utils import run_bass_kernel_spmd

F32 = mybir.dt.float32
BF16 = mybir.dt.bfloat16
AF = mybir.ActivationFunctionType
ALU = mybir.AluOpType

P = 128
T = 4096
EH = 8
H = 1024
NST = 16
J = 1056
FH = 4096
CH = 512
NCHUNK = T // CH
TS = 64
NSUB = CH // TS
TT = 512
NTILE = T // TT
G = 128
EPS = 1e-5


class Sems:

    def __init__(self, nc, stack, names):
        self.h = {n: stack.enter_context(nc.semaphore(n)) for n in names}
        self.cnt = {n: 0 for n in names}
        self.waited = {}

    def inc(self, name, instr, k=1):
        instr.then_inc(self.h[name], k)
        self.cnt[name] += k
        return self.cnt[name]

    def val(self, name):
        return self.cnt[name]

    def wait(self, eng, ename, name, target):
        if target is None or target <= 0:
            return
        key = (ename, name)
        if self.waited.get(key, 0) >= target:
            return
        eng.wait_ge(self.h[name], target)
        self.waited[key] = target


def build_program(phases=4):
    nc = bass.Bass("TRN2", target_bir_lowering=False, debug=False)

    xin = nc.dram_tensor("xin", [H, T], F32, kind="ExternalInput").ap()
    wxt = nc.dram_tensor("wxt", [H, J], F32, kind="ExternalInput").ap()
    w1 = nc.dram_tensor("w1", [H, FH], F32, kind="ExternalInput").ap()
    w2 = nc.dram_tensor("w2", [FH, H], F32, kind="ExternalInput").ap()
    wot = nc.dram_tensor("wot", [H, H], F32, kind="ExternalInput").ap()
    alog = nc.dram_tensor("alog", [1, NST], F32, kind="ExternalInput").ap()
    dvec = nc.dram_tensor("dvec", [P, EH], F32, kind="ExternalInput").ap()
    b1m = nc.dram_tensor("b1m", [P, 32], F32, kind="ExternalInput").ap()
    b2m = nc.dram_tensor("b2m", [P, EH], F32, kind="ExternalInput").ap()
    yout = nc.dram_tensor("yout", [H, T], F32, kind="ExternalOutput").ap()
    a_d = nc.dram_tensor("a_d", [1, NST], F32, kind="Internal").ap()
    ab_d = nc.dram_tensor("ab_d", [2, T], F32, kind="Internal").ap()
    bc_d = nc.dram_tensor("bc_d", [32, T], F32, kind="Internal").ap()
    xn_d = nc.dram_tensor("xn_d", [H, T], BF16, kind="Internal").ap()
    attn_d = nc.dram_tensor("attn_d", [H, T], F32, kind="Internal").ap()
    acca_d = nc.dram_tensor("acca_d", [H, T], F32, kind="Internal").ap()

    xin3 = xin.rearrange("(a p) t -> p a t", p=P)
    xn3 = xn_d.rearrange("(a p) t -> p a t", p=P)
    attn3 = attn_d.rearrange("(a p) t -> p a t", p=P)
    acca3 = acca_d.rearrange("(a p) t -> p a t", p=P)
    yout3 = yout.rearrange("(a p) t -> p a t", p=P)

    with ExitStack() as outer:
        sem = Sems(nc, outer, [
            "lds",
            "sts",
            "ldg",
            "mm",
            "ev",
            "vv",
            "bs", "bg", "bt", "bc_", "bv",
        ])

        def sb(stack, name, shape, dt):
            return stack.enter_context(nc.sbuf_tensor(name, shape, dt))

        A_bc = sb(outer, "A_bc", [P, NST], F32)
        D_sb = sb(outer, "D_sb", [P, EH], F32)
        B1_sb = sb(outer, "B1_sb", [P, 32], F32)
        B2_sb = sb(outer, "B2_sb", [P, EH], F32)
        ones = sb(outer, "ones", [P, 1], F32)
        arow = sb(outer, "arow", [1, NST], F32)

        ps = [outer.enter_context(nc.psum_tensor(f"ps{i}", [P, 512], F32))
              for i in range(6)]
        bank_free = {i: None for i in range(6)}

        block = outer.enter_context(nc.Block())
        barrier_n = [0]

        def barrier():
            barrier_n[0] += 1
            k = barrier_n[0]
            pairs = [("sync", "bs"), ("gpsimd", "bg"), ("tensor", "bt"),
                     ("scalar", "bc_"), ("vector", "bv")]
            for ename, s in pairs:
                @getattr(block, ename)
                def _(eng, s=s):
                    eng.sem_inc(sem.h[s], 1)
                sem.cnt[s] += 1
            for ename, _s in pairs:
                @getattr(block, ename)
                def _(eng, ename=ename):
                    for _, s2 in pairs:
                        sem.wait(eng, ename, s2, k)

        with ExitStack() as s13:
            WxT = sb(s13, "WxT", [P, EH * J], BF16)
            Wot = sb(s13, "Wot", [P, EH * H], BF16)

            @block.sync
            def _(sync):
                sync.dma_start(out=arow[:, :], in_=alog[:, :]).then_inc(sem.h["lds"], 16)
                sem.cnt["lds"] += 16
                sync.dma_start(out=D_sb[:, :], in_=dvec[:, :]).then_inc(sem.h["lds"], 16)
                sem.cnt["lds"] += 16
                sync.dma_start(out=B1_sb[:, :], in_=b1m[:, :]).then_inc(sem.h["lds"], 16)
                sem.cnt["lds"] += 16
                sync.dma_start(out=B2_sb[:, :], in_=b2m[:, :]).then_inc(sem.h["lds"], 16)
                sem.cnt["lds"] += 16
            ph0_lds = sem.val("lds")

            @block.gpsimd
            def _(gp):
                i = gp.dma_start(out=WxT[:, :].rearrange("p (a j) -> p a j", a=EH),
                                 in_=wxt.rearrange("(a p) j -> p a j", p=P))
                sem.inc("ldg", i, 16)
                i = gp.dma_start(out=Wot[:, :].rearrange("p (a e) -> p a e", a=EH),
                                 in_=wot.rearrange("(a p) e -> p a e", p=P))
                sem.inc("ldg", i, 16)
            wx_ld = sem.val("ldg")

            @block.scalar
            def _(sc):
                sem.wait(sc, "scalar", "lds", ph0_lds)
                i = sc.activation(out=arow[:, :], in_=arow[:, :], func=AF.Exp)
                sem.inc("ev", i)
            a_exp = sem.val("ev")

            @block.vector
            def _(v):
                v.memset(ones[:, :], 1.0)
                sem.wait(v, "vector", "ev", a_exp)
                i = v.tensor_scalar(out=arow[:, :], in0=arow[:, :], scalar1=-1.0,
                                    scalar2=None, op0=ALU.mult)
                sem.inc("vv", i)
            a_neg = sem.val("vv")

            @block.sync
            def _(sync):
                sem.wait(sync, "sync", "vv", a_neg)
                i = sync.dma_start(out=a_d[:, :], in_=arow[:, :])
                sem.inc("sts", i, 16)
            a_spill = sem.val("sts")

            @block.gpsimd
            def _(gp):
                sem.wait(gp, "gpsimd", "sts", a_spill)
                i = gp.dma_start(out=A_bc[:, :], in_=a_d[0:1, :].broadcast_to([P, NST]))
                sem.inc("ldg", i, 16)
            abc_ld = sem.val("ldg")

            with ExitStack() as s1:
                xt = [sb(s1, f"xt{b}", [P, EH * TT], F32) for b in range(2)]
                xsq = sb(s1, "xsq", [P, EH * TT], F32)
                xn_t = [sb(s1, f"xnt{b}", [P, EH * TT], BF16) for b in range(2)]
                a_bc = sb(s1, "a_bc", [P, TT], F32)
                b_bc = sb(s1, "b_bc", [P, TT], F32)
                rows = sb(s1, "rows", [1, 512 * 10], F32)

                def row(i):
                    return rows[:, i * 512:(i + 1) * 512]

                x_loaded = []
                xt_free = [None, None]
                xn_free = [None, None]
                xsq_free = None
                abc_free = None
                ab_sp_prev = None
                spill_cnt = []

                for tt in range(NTILE):
                    b = tt % 2
                    t0 = tt * TT

                    @block.sync
                    def _(sync, b=b, t0=t0):
                        if xt_free[b] is not None:
                            sem.wait(sync, "sync", *xt_free[b])
                        i = sync.dma_start(out=xt[b][:, :].rearrange("p (a t) -> p a t", a=EH),
                                           in_=xin3[:, :, t0:t0 + TT])
                        sem.inc("lds", i, 16)
                    x_loaded.append(sem.val("lds"))

                    @block.scalar
                    def _(sc, b=b):
                        sem.wait(sc, "scalar", "lds", x_loaded[-1])
                        if xsq_free is not None:
                            sem.wait(sc, "scalar", *xsq_free)
                        i = sc.activation(out=xsq[:, :], in_=xt[b][:, :], func=AF.Square)
                        sem.inc("ev", i)
                    xsq_done = sem.val("ev")

                    @block.tensor
                    def _(te, b=b):
                        sem.wait(te, "tensor", "lds", x_loaded[-1])
                        sem.wait(te, "tensor", "ev", xsq_done)
                        if bank_free[4] is not None:
                            sem.wait(te, "tensor", *bank_free[4])
                        if bank_free[5] is not None:
                            sem.wait(te, "tensor", *bank_free[5])
                        for kc in range(EH):
                            te.matmul(out=ps[4][0:1, 0:TT], lhsT=ones[:, :],
                                      rhs=xt[b][:, kc * TT:(kc + 1) * TT],
                                      start=(kc == 0), stop=(kc == EH - 1))
                        for kc in range(EH):
                            i = te.matmul(out=ps[5][0:1, 0:TT], lhsT=ones[:, :],
                                          rhs=xsq[:, kc * TT:(kc + 1) * TT],
                                          start=(kc == 0), stop=(kc == EH - 1))
                        sem.inc("mm", i)
                    stats_done = sem.val("mm")
                    xsq_free = ("mm", stats_done)

                    @block.vector
                    def _(v):
                        sem.wait(v, "vector", "mm", stats_done)
                        v.tensor_scalar(out=row(0), in0=ps[4][0:1, 0:TT],
                                        scalar1=1.0 / H, scalar2=None, op0=ALU.mult)
                        v.tensor_scalar(out=row(1), in0=ps[5][0:1, 0:TT],
                                        scalar1=1.0 / H, scalar2=None, op0=ALU.mult)
                        v.tensor_mul(row(2), row(0), row(0))
                        v.tensor_sub(row(3), row(1), row(2))
                        i = v.tensor_scalar(out=row(4), in0=row(3), scalar1=EPS,
                                            scalar2=None, op0=ALU.add)
                        sem.inc("vv", i)
                    veps_done = sem.val("vv")
                    bank_free[4] = ("vv", veps_done)
                    bank_free[5] = ("vv", veps_done)

                    @block.scalar
                    def _(sc):
                        sem.wait(sc, "scalar", "vv", veps_done)
                        i = sc.activation(out=row(5), in_=row(4), func=AF.Sqrt)
                        sem.inc("ev", i)
                    sq_done = sem.val("ev")

                    @block.vector
                    def _(v):
                        sem.wait(v, "vector", "ev", sq_done)
                        if ab_sp_prev is not None:
                            sem.wait(v, "vector", "sts", ab_sp_prev)
                        v.reciprocal(row(6), row(5))
                        v.tensor_mul(row(2), row(6), row(6))
                        v.tensor_mul(row(1), row(2), row(4))
                        v.tensor_scalar(out=row(2), in0=row(1), scalar1=-0.5,
                                        scalar2=1.5, op0=ALU.mult, op1=ALU.add)
                        v.tensor_mul(row(7), row(6), row(2))
                        v.tensor_mul(row(8), row(0), row(7))
                        i = v.tensor_scalar(out=row(8), in0=row(8), scalar1=-1.0,
                                            scalar2=None, op0=ALU.mult)
                        sem.inc("vv", i)
                    ab_done = sem.val("vv")

                    @block.sync
                    def _(sync, t0=t0):
                        sem.wait(sync, "sync", "vv", ab_done)
                        sync.dma_start(out=ab_d[0:1, t0:t0 + TT], in_=row(7)).then_inc(sem.h["sts"], 16)
                        sem.cnt["sts"] += 16
                        i = sync.dma_start(out=ab_d[1:2, t0:t0 + TT], in_=row(8))
                        sem.inc("sts", i, 16)
                    ab_sp = sem.val("sts")
                    ab_sp_prev = ab_sp

                    @block.gpsimd
                    def _(gp, t0=t0):
                        sem.wait(gp, "gpsimd", "sts", ab_sp)
                        if abc_free is not None:
                            sem.wait(gp, "gpsimd", *abc_free)
                        gp.dma_start(out=a_bc[:, :],
                                     in_=ab_d[0:1, t0:t0 + TT].broadcast_to([P, TT])).then_inc(sem.h["ldg"], 16)
                        sem.cnt["ldg"] += 16
                        i = gp.dma_start(out=b_bc[:, :],
                                         in_=ab_d[1:2, t0:t0 + TT].broadcast_to([P, TT]))
                        sem.inc("ldg", i, 16)
                    abc_ld2 = sem.val("ldg")

                    @block.vector
                    def _(v, b=b):
                        sem.wait(v, "vector", "ldg", abc_ld2)
                        if xn_free[b] is not None:
                            sem.wait(v, "vector", *xn_free[b])
                        v.tensor_tensor(
                            out=xt[b][:, :].rearrange("p (a t) -> p a t", a=EH),
                            in0=xt[b][:, :].rearrange("p (a t) -> p a t", a=EH),
                            in1=a_bc[:, :].unsqueeze(1).broadcast_to([P, EH, TT]),
                            op=ALU.mult)
                        i = v.tensor_tensor(
                            out=xn_t[b][:, :].rearrange("p (a t) -> p a t", a=EH),
                            in0=xt[b][:, :].rearrange("p (a t) -> p a t", a=EH),
                            in1=b_bc[:, :].unsqueeze(1).broadcast_to([P, EH, TT]),
                            op=ALU.add)
                        sem.inc("vv", i)
                    xn_done = sem.val("vv")
                    xt_free[b] = ("vv", xn_done)
                    abc_free = ("vv", xn_done)

                    @block.sync
                    def _(sync, b=b, t0=t0):
                        sem.wait(sync, "sync", "vv", xn_done)
                        i = sync.dma_start(out=xn3[:, :, t0:t0 + TT],
                                           in_=xn_t[b][:, :].rearrange("p (a t) -> p a t", a=EH))
                        sem.inc("sts", i, 16)
                    spill_cnt.append(sem.val("sts"))
                    xn_free[b] = ("sts", spill_cnt[-1])

            barrier()

            if phases < 3:
                @block.sync
                def _(sync):
                    sem.wait(sync, "sync", "sts", sem.val("sts"))
                return nc
            with ExitStack() as s3:
                xn_c = [sb(s3, f"xnc{b}", [P, EH * CH], BF16) for b in range(2)]
                delta = sb(s3, "dl", [P, EH * CH], F32)
                dtx = sb(s3, "dtx", [P, EH * CH], BF16)
                dxn = sb(s3, "dxn", [P, EH * CH], BF16)
                dA = [sb(s3, f"dA{b}", [P, G * TS], BF16) for b in range(2)]
                ub = sb(s3, "ub", [P, G * TS], BF16)
                hb = sb(s3, "hb", [P, G * TS], BF16)
                l4f = sb(s3, "l4f", [P, EH * TS], F32)
                yb = [sb(s3, f"yb{b}", [P, EH * CH], BF16) for b in range(2)]
                att_t = sb(s3, "att_t", [P, EH * CH], F32)
                carry = sb(s3, "carry", [P, G], F32)
                dA0 = sb(s3, "dA0v", [P, G], F32)
                tmpg = sb(s3, "tmpg", [P, G], F32)
                BCs = sb(s3, "BCs", [32, CH], F32)
                Bb = [sb(s3, f"Bb{b}", [P, NST * TS], BF16) for b in range(2)]
                Cb = [sb(s3, f"Cb{b}", [P, NST * TS], BF16) for b in range(2)]

                @block.vector
                def _(v):
                    v.memset(carry[:, :], 0.0)

                @block.sync
                def _(sync):
                    sem.wait(sync, "sync", "sts", spill_cnt[0])
                    i = sync.dma_start(out=xn_c[0][:, :].rearrange("p (a t) -> p a t", a=EH),
                                       in_=xn3[:, :, 0:CH])
                    sem.inc("lds", i, 16)
                xn_ld = [sem.val("lds")]

                sub_done = []
                y_done = []
                wout_ev = []
                att_sp = []
                xnc_free = [None, None]
                delta_free = [None, None]
                y_free = [None, None]
                dlb_evcnt = {}

                def emit_wout(cc):
                    bcy = cc % 2
                    evw_base = sem.val("ev")
                    for eb in range(EH):
                        bank = 2 + eb % 2

                        @block.tensor
                        def _(te, cc=cc, bcy=bcy, eb=eb, bank=bank):
                            sem.wait(te, "tensor", "vv", y_done[cc])
                            sem.wait(te, "tensor", "ldg", wx_ld)
                            if eb < 2:
                                if bank_free[bank] is not None:
                                    sem.wait(te, "tensor", *bank_free[bank])
                            else:
                                sem.wait(te, "tensor", "ev", evw_base + eb - 1)
                            for kc in range(EH):
                                i = te.matmul(
                                    out=ps[bank][:, 0:CH],
                                    lhsT=Wot[:, kc * H + eb * P:kc * H + (eb + 1) * P],
                                    rhs=yb[bcy][:, kc * CH:(kc + 1) * CH],
                                    start=(kc == 0), stop=(kc == EH - 1))
                            sem.inc("mm", i)
                            dlb_evcnt[("wo", cc, eb)] = sem.val("mm")

                    @block.scalar
                    def _(sc, cc=cc):
                        if att_sp:
                            sem.wait(sc, "scalar", "sts", att_sp[-1])
                        for eb in range(EH):
                            bank = 2 + eb % 2
                            sem.wait(sc, "scalar", "mm", dlb_evcnt[("wo", cc, eb)])
                            i = sc.activation(out=att_t[:, eb * CH:(eb + 1) * CH],
                                              in_=ps[bank][:, 0:CH], func=AF.Copy)
                            sem.inc("ev", i)
                    assert sem.val("ev") == evw_base + EH
                    bank_free[2] = ("ev", evw_base + EH - 1)
                    bank_free[3] = ("ev", evw_base + EH)
                    wout_ev.append(sem.val("ev"))

                    @block.sync
                    def _(sync, cc=cc):
                        sem.wait(sync, "sync", "ev", wout_ev[-1])
                        i = sync.dma_start(out=attn3[:, :, cc * CH:(cc + 1) * CH],
                                           in_=att_t[:, :].rearrange("p (a t) -> p a t", a=EH))
                        sem.inc("sts", i, 16)
                    att_sp.append(sem.val("sts"))
                    y_free[cc % 2] = ("mm", dlb_evcnt[("wo", cc, EH - 1)])

                for cc in range(NCHUNK):
                    bcc = cc % 2
                    ct0 = cc * CH

                    if cc + 1 < NCHUNK:
                        @block.sync
                        def _(sync, cc=cc):
                            nb = (cc + 1) % 2
                            sem.wait(sync, "sync", "sts", spill_cnt[cc + 1])
                            if xnc_free[nb] is not None:
                                sem.wait(sync, "sync", *xnc_free[nb])
                            i = sync.dma_start(
                                out=xn_c[nb][:, :].rearrange("p (a t) -> p a t", a=EH),
                                in_=xn3[:, :, (cc + 1) * CH:(cc + 2) * CH])
                            sem.inc("lds", i, 16)
                        xn_ld.append(sem.val("lds"))

                    ev_base = sem.val("ev")
                    for jb in [8] + list(range(8)):
                        bank = 4 if jb == 8 else jb % 2
                        mwid = 32 if jb == 8 else P

                        @block.tensor
                        def _(te, cc=cc, bcc=bcc, jb=jb, bank=bank, mwid=mwid):
                            sem.wait(te, "tensor", "lds", xn_ld[cc])
                            sem.wait(te, "tensor", "ldg", wx_ld)
                            if jb in (8, 0, 1):
                                if bank_free[bank] is not None:
                                    sem.wait(te, "tensor", *bank_free[bank])
                            else:
                                sem.wait(te, "tensor", "ev", ev_base + jb)
                            for kc in range(EH):
                                i = te.matmul(
                                    out=ps[bank][0:mwid, 0:CH],
                                    lhsT=WxT[:, kc * J + jb * P:kc * J + jb * P + mwid],
                                    rhs=xn_c[bcc][:, kc * CH:(kc + 1) * CH],
                                    start=(kc == 0), stop=(kc == EH - 1))
                            sem.inc("mm", i)
                            dlb_evcnt[(cc, jb)] = sem.val("mm")

                    @block.scalar
                    def _(sc, cc=cc):
                        sem.wait(sc, "scalar", "mm", dlb_evcnt[(cc, 8)])
                        if cc >= 1:
                            sem.wait(sc, "scalar", "sts", dlb_evcnt[(cc - 1, "bcsp")])
                        i = sc.activation(out=BCs[:, :], in_=ps[4][0:32, 0:CH],
                                          func=AF.Copy)
                        sem.inc("ev", i)
                        dlb_evcnt[(cc, "bc")] = sem.val("ev")
                        if delta_free[0] is not None:
                            sem.wait(sc, "scalar", *delta_free[0])
                        for jb in range(8):
                            bank = jb % 2
                            sem.wait(sc, "scalar", "mm", dlb_evcnt[(cc, jb)])
                            sl = delta[:, jb * CH:(jb + 1) * CH]
                            sc.activation(out=sl, in_=ps[bank][:, 0:CH], func=AF.Exp)
                            i = sc.activation(out=sl, in_=sl, func=AF.Ln, bias=1.0)
                            sem.inc("ev", i)
                    assert sem.val("ev") == ev_base + 9
                    bank_free[4] = ("ev", ev_base + 1)
                    bank_free[0] = ("ev", ev_base + 8)
                    bank_free[1] = ("ev", ev_base + 9)
                    dlb_evcnt[(cc, "dl")] = sem.val("ev")

                    @block.sync
                    def _(sync, cc=cc, ct0=ct0):
                        sem.wait(sync, "sync", "ev", dlb_evcnt[(cc, "bc")])
                        i = sync.dma_start(out=bc_d[:, ct0:ct0 + CH], in_=BCs[:, :])
                        sem.inc("sts", i, 16)
                    dlb_evcnt[(cc, "bcsp")] = sem.val("sts")

                    @block.vector
                    def _(v, cc=cc, bcc=bcc):
                        sem.wait(v, "vector", "ev", dlb_evcnt[(cc, "dl")])
                        sem.wait(v, "vector", "lds", xn_ld[cc])
                        v.tensor_tensor(out=dtx[:, :], in0=delta[:, :],
                                        in1=xn_c[bcc][:, :], op=ALU.mult)
                        for chx in range(EH):
                            i = v.tensor_scalar(out=dxn[:, chx * CH:(chx + 1) * CH],
                                                in0=xn_c[bcc][:, chx * CH:(chx + 1) * CH],
                                                scalar1=D_sb[:, chx:chx + 1],
                                                scalar2=None, op0=ALU.mult)
                        sem.inc("vv", i)
                    dtx_done = sem.val("vv")
                    xnc_free[bcc] = ("vv", dtx_done)

                    for s in range(NSUB):
                        gs = cc * NSUB + s
                        bs_ = s % 2
                        ts0 = s * TS
                        gt0 = ct0 + ts0

                        @block.gpsimd
                        def _(gp, gs=gs, bs_=bs_, gt0=gt0, cc=cc):
                            sem.wait(gp, "gpsimd", "sts", dlb_evcnt[(cc, "bcsp")])
                            if gs >= 2:
                                sem.wait(gp, "gpsimd", "vv", sub_done[gs - 2])
                            gp.dma_start(
                                out=Bb[bs_][:, :].rearrange("p (n t) -> p n t", n=NST),
                                in_=bc_d[0:NST, gt0:gt0 + TS].unsqueeze(0).broadcast_to([P, NST, TS])
                            ).then_inc(sem.h["ldg"], 16)
                            sem.cnt["ldg"] += 16
                            i = gp.dma_start(
                                out=Cb[bs_][:, :].rearrange("p (n t) -> p n t", n=NST),
                                in_=bc_d[NST:32, gt0:gt0 + TS].unsqueeze(0).broadcast_to([P, NST, TS]))
                            sem.inc("ldg", i, 16)
                        bc_ld = sem.val("ldg")

                        @block.scalar
                        def _(sc, gs=gs, bs_=bs_, ts0=ts0, bcc=bcc):
                            if gs >= 2:
                                sem.wait(sc, "scalar", "vv", sub_done[gs - 2])
                            d3 = delta[:, :].rearrange("p (a t) -> p a t", a=EH)[:, :, ts0:ts0 + TS]
                            for n in range(NST):
                                i = sc.activation(
                                    out=dA[bs_][:, n * EH * TS:(n + 1) * EH * TS]
                                        .rearrange("p (a t) -> p a t", a=EH),
                                    in_=d3, func=AF.Exp, scale=A_bc[:, n:n + 1])
                            sem.inc("ev", i)
                        exp_done = sem.val("ev")

                        yfree = y_free[cc % 2] if s == 0 else None

                        @block.vector
                        def _(v, gs=gs, bs_=bs_, ts0=ts0, yfree=yfree):
                            sem.wait(v, "vector", "ldg", bc_ld)
                            sem.wait(v, "vector", "ev", exp_done)
                            if yfree is not None:
                                sem.wait(v, "vector", *yfree)
                            d4 = dtx[:, :].rearrange("p (a t) -> p a t", a=EH)[:, :, ts0:ts0 + TS]
                            u4 = ub[:, :].rearrange("p (n a t) -> p n a t", n=NST, a=EH)
                            v.tensor_tensor(
                                out=u4,
                                in0=d4.unsqueeze(1).broadcast_to([P, NST, EH, TS]),
                                in1=Bb[bs_][:, :].rearrange("p (n t) -> p n t", n=NST)
                                    .unsqueeze(2).broadcast_to([P, NST, EH, TS]),
                                op=ALU.mult)
                            dA3 = dA[bs_][:, :].rearrange("p (g t) -> p g t", g=G)
                            v.tensor_copy(out=dA0[:, :].unsqueeze(2), in_=dA3[:, :, 0:1])
                            v.tensor_mul(tmpg[:, :], dA0[:, :], carry[:, :])
                            u3 = ub[:, :].rearrange("p (g t) -> p g t", g=G)
                            v.tensor_tensor(out=u3[:, :, 0:1], in0=u3[:, :, 0:1],
                                            in1=tmpg[:, :].unsqueeze(2), op=ALU.add)
                            v.memset(dA3[:, :, 0:1], 0.0)
                            v.tensor_tensor_scan(out=hb[:, :], data0=dA[bs_][:, :],
                                                 data1=ub[:, :], initial=0.0,
                                                 op0=ALU.mult, op1=ALU.add)
                            h3 = hb[:, :].rearrange("p (g t) -> p g t", g=G)
                            v.tensor_copy(out=carry[:, :].unsqueeze(2), in_=h3[:, :, TS - 1:TS])
                            v.tensor_tensor(
                                out=u4, in0=hb[:, :].rearrange("p (n a t) -> p n a t", n=NST, a=EH),
                                in1=Cb[bs_][:, :].rearrange("p (n t) -> p n t", n=NST)
                                    .unsqueeze(2).broadcast_to([P, NST, EH, TS]),
                                op=ALU.mult)
                            half = G * TS // 2
                            v.tensor_tensor(out=hb[:, 0:half], in0=ub[:, 0:half],
                                            in1=ub[:, half:2 * half], op=ALU.add)
                            q = half // 2
                            v.tensor_tensor(out=hb[:, half:half + q], in0=hb[:, 0:q],
                                            in1=hb[:, q:2 * q], op=ALU.add)
                            e8 = q // 2
                            v.tensor_tensor(out=hb[:, half + q:half + q + e8],
                                            in0=hb[:, half:half + e8],
                                            in1=hb[:, half + e8:half + 2 * e8], op=ALU.add)
                            e16 = e8 // 2
                            v.tensor_tensor(out=l4f[:, :], in0=hb[:, half + q:half + q + e16],
                                            in1=hb[:, half + q + e16:half + q + 2 * e16], op=ALU.add)
                            i = v.tensor_tensor(
                                out=yb[gs // NSUB % 2][:, :].rearrange("p (a t) -> p a t", a=EH)[:, :, ts0:ts0 + TS],
                                in0=l4f[:, :].rearrange("p (a t) -> p a t", a=EH),
                                in1=dxn[:, :].rearrange("p (a t) -> p a t", a=EH)[:, :, ts0:ts0 + TS],
                                op=ALU.add)
                            sem.inc("vv", i)
                        sub_done.append(sem.val("vv"))

                    y_done.append(sub_done[-1])
                    delta_free[0] = ("vv", sub_done[-1])
                    if cc + 2 < NCHUNK:
                        pass
                    emit_wout(cc)

            barrier()

        barrier()

        if phases < 4:
            @block.sync
            def _(sync):
                sem.wait(sync, "sync", "sts", sem.val("sts"))
            return nc
        with ExitStack() as s4:
            w1h = sb(s4, "w1h", [P, EH * (FH // 2)], BF16)
            w2h = sb(s4, "w2h", [P, (FH // 256) * H], BF16)
            xn_t4 = [sb(s4, f"xn4{b}", [P, EH * TT], BF16) for b in range(2)]
            res_t = [sb(s4, f"res{b}", [P, EH * TT], F32) for b in range(2)]
            att_t4 = [sb(s4, "at4", [P, EH * TT], F32)] * 2
            h_bf = sb(s4, "h_bf", [P, 16 * TT], BF16)
            fB = sb(s4, "fB", [P, EH * TT], F32)
            out_t = [sb(s4, "out1", [P, EH * TT], F32)] * 2

            HHALF = FH // 2

            for pas in range(2):
                hoff = pas * HHALF

                @block.gpsimd
                def _(gp, pas=pas, hoff=hoff):
                    if pas == 1:
                        sem.wait(gp, "gpsimd", "mm", w_used[0])
                    gp.dma_start(
                        out=w1h[:, :].rearrange("p (a h) -> p a h", a=EH),
                        in_=w1.rearrange("(a p) h -> p a h", p=P)[:, :, hoff:hoff + HHALF]
                    ).then_inc(sem.h["ldg"], 16)
                    sem.cnt["ldg"] += 16
                    i = gp.dma_start(
                        out=w2h[:, :].rearrange("p (a e) -> p a e", a=16),
                        in_=w2.rearrange("(a p) e -> p a e", p=P)[:, pas * 16:(pas + 1) * 16, :])
                    sem.inc("ldg", i, 16)
                w_ld = sem.val("ldg")
                w_used = [None]

                gelu_cnt = {}
                ffn2_cnt = {}
                res_free = [None, None]
                xn4_free = [None, None]
                att4_free = [None, None]
                out_free = [None, None]
                h_free = None
                fin_prev = None

                for tt in range(NTILE):
                    b = tt % 2
                    t0 = tt * TT

                    @block.sync
                    def _(sync, pas=pas, b=b, t0=t0):
                        if xn4_free[b] is not None:
                            sem.wait(sync, "sync", *xn4_free[b])
                        sync.dma_start(out=xn_t4[b][:, :].rearrange("p (a t) -> p a t", a=EH),
                                       in_=xn3[:, :, t0:t0 + TT]).then_inc(sem.h["lds"], 16)
                        sem.cnt["lds"] += 16
                        if res_free[b] is not None:
                            sem.wait(sync, "sync", *res_free[b])
                        src = xin3 if pas == 0 else acca3
                        i = sync.dma_start(out=res_t[b][:, :].rearrange("p (a t) -> p a t", a=EH),
                                           in_=src[:, :, t0:t0 + TT])
                        sem.inc("lds", i, 16)
                        if pas == 1:
                            if att4_free[0] is not None:
                                sem.wait(sync, "sync", *att4_free[0])
                            i = sync.dma_start(out=att_t4[b][:, :].rearrange("p (a t) -> p a t", a=EH),
                                               in_=attn3[:, :, t0:t0 + TT])
                            sem.inc("lds", i, 16)
                    t_ld = sem.val("lds")

                    evt_base = sem.val("ev")
                    for hb_ in range(16):
                        bank = hb_ % 2

                        @block.tensor
                        def _(te, pas=pas, b=b, tt=tt, hb_=hb_, bank=bank):
                            sem.wait(te, "tensor", "lds", t_ld)
                            sem.wait(te, "tensor", "ldg", w_ld)
                            if hb_ == 0 and h_free is not None:
                                sem.wait(te, "tensor", *h_free)
                            if hb_ < 2:
                                if bank_free[bank] is not None:
                                    sem.wait(te, "tensor", *bank_free[bank])
                            else:
                                sem.wait(te, "tensor", "ev", evt_base + hb_ - 1)
                            for kc in range(EH):
                                i = te.matmul(
                                    out=ps[bank][:, 0:TT],
                                    lhsT=w1h[:, kc * HHALF + hb_ * P:kc * HHALF + (hb_ + 1) * P],
                                    rhs=xn_t4[b][:, kc * TT:(kc + 1) * TT],
                                    start=(kc == 0), stop=(kc == EH - 1))
                            sem.inc("mm", i)
                            gelu_cnt[(pas, tt, hb_)] = sem.val("mm")

                    @block.scalar
                    def _(sc, pas=pas, tt=tt):
                        for hb_ in range(16):
                            bank = hb_ % 2
                            sem.wait(sc, "scalar", "mm", gelu_cnt[(pas, tt, hb_)])
                            i = sc.activation(out=h_bf[:, hb_ * TT:(hb_ + 1) * TT],
                                              in_=ps[bank][:, 0:TT], func=AF.Gelu,
                                              bias=B1_sb[:, pas * 16 + hb_:pas * 16 + hb_ + 1])
                            sem.inc("ev", i)
                    assert sem.val("ev") == evt_base + 16
                    bank_free[0] = ("ev", evt_base + 15)
                    bank_free[1] = ("ev", evt_base + 16)
                    gelu_done = sem.val("ev")

                    evq_base = sem.val("vv") if pas == 0 else sem.val("ev")
                    for eb in range(EH):
                        bank = 2 + eb % 2

                        @block.tensor
                        def _(te, pas=pas, tt=tt, eb=eb, bank=bank):
                            sem.wait(te, "tensor", "ev", gelu_done)
                            if eb < 2:
                                if bank_free[bank] is not None:
                                    sem.wait(te, "tensor", *bank_free[bank])
                            else:
                                sem.wait(te, "tensor", "vv" if pas == 0 else "ev",
                                         evq_base + eb - 1)
                            for kc in range(16):
                                i = te.matmul(
                                    out=ps[bank][:, 0:TT],
                                    lhsT=w2h[:, kc * H + eb * P:kc * H + (eb + 1) * P],
                                    rhs=h_bf[:, kc * TT:(kc + 1) * TT],
                                    start=(kc == 0), stop=(kc == 15))
                            sem.inc("mm", i)
                            ffn2_cnt[(pas, tt, eb)] = sem.val("mm")

                    if pas == 0:
                        @block.vector
                        def _(v, tt=tt, b=b):
                            if out_free[0] is not None:
                                sem.wait(v, "vector", *out_free[0])
                            for eb in range(EH):
                                bank = 2 + eb % 2
                                sem.wait(v, "vector", "mm", ffn2_cnt[(0, tt, eb)])
                                i = v.tensor_tensor(out=out_t[b][:, eb * TT:(eb + 1) * TT],
                                                    in0=ps[bank][:, 0:TT],
                                                    in1=res_t[b][:, eb * TT:(eb + 1) * TT],
                                                    op=ALU.add)
                                sem.inc("vv", i)
                        assert sem.val("vv") == evq_base + EH
                        bank_free[2] = ("vv", evq_base + EH - 1)
                        bank_free[3] = ("vv", evq_base + EH)
                    else:
                        @block.scalar
                        def _(sc, tt=tt):
                            if fin_prev is not None:
                                sem.wait(sc, "scalar", "vv", fin_prev)
                            for eb in range(EH):
                                bank = 2 + eb % 2
                                sem.wait(sc, "scalar", "mm", ffn2_cnt[(1, tt, eb)])
                                i = sc.activation(out=fB[:, eb * TT:(eb + 1) * TT],
                                                  in_=ps[bank][:, 0:TT], func=AF.Identity,
                                                  bias=B2_sb[:, eb:eb + 1])
                                sem.inc("ev", i)
                        assert sem.val("ev") == evq_base + EH
                        bank_free[2] = ("ev", evq_base + EH - 1)
                        bank_free[3] = ("ev", evq_base + EH)

                    h_free = ("mm", ffn2_cnt[(pas, tt, EH - 1)])
                    if tt == NTILE - 1:
                        w_used[0] = ffn2_cnt[(pas, tt, EH - 1)]

                    if pas == 0:
                        acc_done = sem.val("vv")
                        res_free[b] = ("vv", acc_done)
                        xn4_free[b] = h_free

                        @block.sync
                        def _(sync, tt=tt, b=b, t0=t0):
                            sem.wait(sync, "sync", "vv", acc_done)
                            i = sync.dma_start(out=acca3[:, :, t0:t0 + TT],
                                               in_=out_t[b][:, :].rearrange("p (a t) -> p a t", a=EH))
                            sem.inc("sts", i, 16)
                        out_free[0] = ("sts", sem.val("sts"))
                    else:
                        fB_done = sem.val("ev")

                        @block.vector
                        def _(v, tt=tt, b=b):
                            sem.wait(v, "vector", "ev", fB_done)
                            if out_free[0] is not None:
                                sem.wait(v, "vector", *out_free[0])
                            v.tensor_tensor(out=out_t[b][:, :], in0=fB[:, :],
                                            in1=res_t[b][:, :], op=ALU.add)
                            i = v.tensor_tensor(out=out_t[b][:, :], in0=out_t[b][:, :],
                                                in1=att_t4[b][:, :], op=ALU.add)
                            sem.inc("vv", i)
                        fin_done = sem.val("vv")
                        fin_prev = fin_done
                        res_free[b] = ("vv", fin_done)
                        att4_free[0] = ("vv", fin_done)
                        xn4_free[b] = h_free

                        @block.sync
                        def _(sync, tt=tt, b=b, t0=t0):
                            sem.wait(sync, "sync", "vv", fin_done)
                            i = sync.dma_start(out=yout3[:, :, t0:t0 + TT],
                                               in_=out_t[b][:, :].rearrange("p (a t) -> p a t", a=EH))
                            sem.inc("sts", i, 16)
                        out_free[0] = ("sts", sem.val("sts"))

            @block.sync
            def _(sync):
                sem.wait(sync, "sync", "sts", sem.val("sts"))

    return nc


def kernel(L0_attn_x, L0_attn_Wx, L0_attn_A_log, L0_attn_D, L0_attn_Wout,
           L0_ffn_w1, L0_ffn_b1, L0_ffn_w2, L0_ffn_b2):
    x = np.asarray(L0_attn_x, dtype=np.float32)
    Wx = np.asarray(L0_attn_Wx, dtype=np.float32)
    A_log = np.asarray(L0_attn_A_log, dtype=np.float32)
    D = np.asarray(L0_attn_D, dtype=np.float32)
    Wout = np.asarray(L0_attn_Wout, dtype=np.float32)
    w1 = np.asarray(L0_ffn_w1, dtype=np.float32)
    b1 = np.asarray(L0_ffn_b1, dtype=np.float32)
    w2 = np.asarray(L0_ffn_w2, dtype=np.float32)
    b2 = np.asarray(L0_ffn_b2, dtype=np.float32)

    wxt = np.ascontiguousarray(np.concatenate([Wx[32:], Wx[:32]], axis=0).T)
    wot = np.ascontiguousarray(Wout.T)
    w1c = np.ascontiguousarray(w1)
    w2c = np.ascontiguousarray(w2)
    alog = np.ascontiguousarray(A_log.reshape(1, 16))
    dvec = np.ascontiguousarray(D.reshape(8, 128).T)
    b1m = np.ascontiguousarray(b1.reshape(32, 128).T)
    b2m = np.ascontiguousarray(b2.reshape(8, 128).T)

    nc = build_program()
    in_maps = []
    for i in range(8):
        in_maps.append({
            "xin": np.ascontiguousarray(x[i].T),
            "wxt": wxt, "w1": w1c, "w2": w2c, "wot": wot,
            "alog": alog, "dvec": dvec, "b1m": b1m, "b2m": b2m,
        })
    res = run_bass_kernel_spmd(nc, in_maps, core_ids=list(range(8)))
    out = np.stack([np.asarray(r["yout"], dtype=np.float32).T for r in res.results], axis=0)
    return np.ascontiguousarray(out)


# version 06
# speedup vs baseline: 1.0578x; 1.0355x over previous
import numpy as np
from contextlib import ExitStack

import concourse.bass as bass
import concourse.mybir as mybir
from concourse.bass_utils import run_bass_kernel_spmd

F32 = mybir.dt.float32
BF16 = mybir.dt.bfloat16
AF = mybir.ActivationFunctionType
ALU = mybir.AluOpType

P = 128
T = 4096
EH = 8
H = 1024
NST = 16
J = 1056
FH = 4096
CH = 512
NCHUNK = T // CH
TS = 64
NSUB = CH // TS
TT = 512
NTILE = T // TT
G = 128
EPS = 1e-5


class Sems:

    def __init__(self, nc, stack, names):
        self.h = {n: stack.enter_context(nc.semaphore(n)) for n in names}
        self.cnt = {n: 0 for n in names}
        self.waited = {}

    def inc(self, name, instr, k=1):
        instr.then_inc(self.h[name], k)
        self.cnt[name] += k
        return self.cnt[name]

    def val(self, name):
        return self.cnt[name]

    def wait(self, eng, ename, name, target):
        if target is None or target <= 0:
            return
        key = (ename, name)
        if self.waited.get(key, 0) >= target:
            return
        eng.wait_ge(self.h[name], target)
        self.waited[key] = target


def build_program(phases=4, dbg=False):
    SCR = "ExternalOutput" if dbg else "Internal"
    nc = bass.Bass("TRN2", target_bir_lowering=False, debug=False)

    xin = nc.dram_tensor("xin", [H, T], F32, kind="ExternalInput").ap()
    wxt = nc.dram_tensor("wxt", [H, J], F32, kind="ExternalInput").ap()
    w1 = nc.dram_tensor("w1", [H, FH], F32, kind="ExternalInput").ap()
    w2 = nc.dram_tensor("w2", [FH, H], F32, kind="ExternalInput").ap()
    wot = nc.dram_tensor("wot", [H, H], F32, kind="ExternalInput").ap()
    alog = nc.dram_tensor("alog", [1, NST], F32, kind="ExternalInput").ap()
    dvec = nc.dram_tensor("dvec", [P, EH], F32, kind="ExternalInput").ap()
    b1m = nc.dram_tensor("b1m", [P, 32], F32, kind="ExternalInput").ap()
    b2m = nc.dram_tensor("b2m", [P, EH], F32, kind="ExternalInput").ap()
    yout = nc.dram_tensor("yout", [H, T], F32, kind="ExternalOutput").ap()
    a_d = nc.dram_tensor("a_d", [1, NST], F32, kind="Internal").ap()
    ab_d = nc.dram_tensor("ab_d", [2, T], F32, kind=SCR).ap()
    bc_d = nc.dram_tensor("bc_d", [32, T], F32, kind=SCR).ap()
    xn_d = nc.dram_tensor("xn_d", [H, T], BF16, kind=SCR).ap()
    delta_d = nc.dram_tensor("delta_d", [H, T], BF16, kind=SCR).ap()
    attn_d = nc.dram_tensor("attn_d", [H, T], F32, kind=SCR).ap()
    acca_d = nc.dram_tensor("acca_d", [H, T], F32, kind=SCR).ap()

    xin3 = xin.rearrange("(a p) t -> p a t", p=P)
    xn3 = xn_d.rearrange("(a p) t -> p a t", p=P)
    delta3 = delta_d.rearrange("(a p) t -> p a t", p=P)
    attn3 = attn_d.rearrange("(a p) t -> p a t", p=P)
    acca3 = acca_d.rearrange("(a p) t -> p a t", p=P)
    yout3 = yout.rearrange("(a p) t -> p a t", p=P)

    with ExitStack() as outer:
        sem = Sems(nc, outer, [
            "lds",
            "sts",
            "ldg",
            "mm",
            "ev",
            "vv",
            "bs", "bg", "bt", "bc_", "bv",
        ])

        def sb(stack, name, shape, dt):
            return stack.enter_context(nc.sbuf_tensor(name, shape, dt))

        A_bc = sb(outer, "A_bc", [P, NST], F32)
        D_sb = sb(outer, "D_sb", [P, EH], F32)
        B1_sb = sb(outer, "B1_sb", [P, 32], F32)
        B2_sb = sb(outer, "B2_sb", [P, EH], F32)
        ones = sb(outer, "ones", [P, 1], F32)
        arow = sb(outer, "arow", [1, NST], F32)

        ps = [outer.enter_context(nc.psum_tensor(f"ps{i}", [P, 512], F32))
              for i in range(6)]
        bank_free = {i: None for i in range(6)}

        block = outer.enter_context(nc.Block())
        barrier_n = [0]

        def barrier():
            barrier_n[0] += 1
            k = barrier_n[0]
            pairs = [("sync", "bs"), ("gpsimd", "bg"), ("tensor", "bt"),
                     ("scalar", "bc_"), ("vector", "bv")]
            for ename, s in pairs:
                @getattr(block, ename)
                def _(eng, s=s):
                    eng.sem_inc(sem.h[s], 1)
                sem.cnt[s] += 1
            for ename, _s in pairs:
                @getattr(block, ename)
                def _(eng, ename=ename):
                    for _, s2 in pairs:
                        sem.wait(eng, ename, s2, k)

        with ExitStack() as s13:
            Wot = sb(s13, "Wot", [P, EH * H], BF16)

            @block.sync
            def _(sync):
                sync.dma_start(out=arow[:, :], in_=alog[:, :]).then_inc(sem.h["lds"], 16)
                sem.cnt["lds"] += 16
                sync.dma_start(out=D_sb[:, :], in_=dvec[:, :]).then_inc(sem.h["lds"], 16)
                sem.cnt["lds"] += 16
                sync.dma_start(out=B1_sb[:, :], in_=b1m[:, :]).then_inc(sem.h["lds"], 16)
                sem.cnt["lds"] += 16
                sync.dma_start(out=B2_sb[:, :], in_=b2m[:, :]).then_inc(sem.h["lds"], 16)
                sem.cnt["lds"] += 16
            ph0_lds = sem.val("lds")

            @block.gpsimd
            def _(gp):
                i = gp.dma_start(out=Wot[:, :].rearrange("p (a e) -> p a e", a=EH),
                                 in_=wot.rearrange("(a p) e -> p a e", p=P))
                sem.inc("ldg", i, 16)
            wx_ld = sem.val("ldg")

            @block.scalar
            def _(sc):
                sem.wait(sc, "scalar", "lds", ph0_lds)
                i = sc.activation(out=arow[:, :], in_=arow[:, :], func=AF.Exp)
                sem.inc("ev", i)
            a_exp = sem.val("ev")

            @block.vector
            def _(v):
                v.memset(ones[:, :], 1.0)
                sem.wait(v, "vector", "ev", a_exp)
                i = v.tensor_scalar(out=arow[:, :], in0=arow[:, :], scalar1=-1.0,
                                    scalar2=None, op0=ALU.mult)
                sem.inc("vv", i)
            a_neg = sem.val("vv")

            @block.sync
            def _(sync):
                sem.wait(sync, "sync", "vv", a_neg)
                i = sync.dma_start(out=a_d[:, :], in_=arow[:, :])
                sem.inc("sts", i, 16)
            a_spill = sem.val("sts")

            @block.gpsimd
            def _(gp):
                sem.wait(gp, "gpsimd", "sts", a_spill)
                i = gp.dma_start(out=A_bc[:, :], in_=a_d[0:1, :].broadcast_to([P, NST]))
                sem.inc("ldg", i, 16)
            abc_ld = sem.val("ldg")

            with ExitStack() as s1:
                WxT = sb(s1, "WxT", [P, EH * J], BF16)
                xt = [sb(s1, f"xt{b}", [P, EH * TT], F32) for b in range(2)]
                xsq = sb(s1, "xsq", [P, EH * TT], F32)
                xn_t = [sb(s1, f"xnt{b}", [P, EH * TT], BF16) for b in range(2)]
                dl_t = [sb(s1, f"dlt{b}", [P, EH * TT], BF16) for b in range(2)]
                BCs = sb(s1, "BCs", [32, TT], F32)
                a_bc = sb(s1, "a_bc", [P, TT], F32)
                b_bc = sb(s1, "b_bc", [P, TT], F32)
                rows = sb(s1, "rows", [1, 512 * 10], F32)

                def row(i):
                    return rows[:, i * 512:(i + 1) * 512]

                @block.gpsimd
                def _(gp):
                    i = gp.dma_start(out=WxT[:, :].rearrange("p (a j) -> p a j", a=EH),
                                     in_=wxt.rearrange("(a p) j -> p a j", p=P))
                    sem.inc("ldg", i, 16)
                wxt_ld = sem.val("ldg")

                x_loaded = []
                xt_free = [None, None]
                xn_free = [None, None]
                xsq_free = None
                abc_free = None
                ab_sp_prev = None
                bc_sp_prev = None
                bc_sp = []
                dl_free = [None, None]
                spill_cnt = []
                dl_spill = []
                dlb_evcnt = {}

                spsc = sb(s1, "spsc", [P, TT], F32)
                xn_done_t = {}

                def stage_a(tt):
                    nonlocal abc_free, ab_sp_prev
                    b = tt % 2
                    t0 = tt * TT

                    @block.sync
                    def _(sync, b=b, t0=t0):
                        if xt_free[b] is not None:
                            sem.wait(sync, "sync", *xt_free[b])
                        i = sync.dma_start(out=xt[b][:, :].rearrange("p (a t) -> p a t", a=EH),
                                           in_=xin3[:, :, t0:t0 + TT])
                        sem.inc("lds", i, 16)
                    x_loaded.append(sem.val("lds"))

                    @block.scalar
                    def _(sc, b=b):
                        sem.wait(sc, "scalar", "lds", x_loaded[-1])
                        if xsq_free[0] is not None:
                            sem.wait(sc, "scalar", *xsq_free[0])
                        i = sc.activation(out=xsq[:, :], in_=xt[b][:, :], func=AF.Square)
                        sem.inc("ev", i)
                    xsq_done = sem.val("ev")

                    @block.tensor
                    def _(te, b=b):
                        sem.wait(te, "tensor", "lds", x_loaded[-1])
                        sem.wait(te, "tensor", "ev", xsq_done)
                        if bank_free[4] is not None:
                            sem.wait(te, "tensor", *bank_free[4])
                        if bank_free[5] is not None:
                            sem.wait(te, "tensor", *bank_free[5])
                        for kc in range(EH):
                            te.matmul(out=ps[4][0:1, 0:TT], lhsT=ones[:, :],
                                      rhs=xt[b][:, kc * TT:(kc + 1) * TT],
                                      start=(kc == 0), stop=(kc == EH - 1))
                        for kc in range(EH):
                            i = te.matmul(out=ps[5][0:1, 0:TT], lhsT=ones[:, :],
                                          rhs=xsq[:, kc * TT:(kc + 1) * TT],
                                          start=(kc == 0), stop=(kc == EH - 1))
                        sem.inc("mm", i)
                    stats_done = sem.val("mm")
                    xsq_free[0] = ("mm", stats_done)

                    @block.vector
                    def _(v):
                        sem.wait(v, "vector", "mm", stats_done)
                        if ab_sp_prev is not None:
                            sem.wait(v, "vector", "sts", ab_sp_prev)
                        v.tensor_scalar(out=row(0), in0=ps[4][0:1, 0:TT],
                                        scalar1=1.0 / H, scalar2=None, op0=ALU.mult)
                        v.tensor_scalar(out=row(1), in0=ps[5][0:1, 0:TT],
                                        scalar1=1.0 / H, scalar2=None, op0=ALU.mult)
                        v.tensor_mul(row(2), row(0), row(0))
                        v.tensor_sub(row(3), row(1), row(2))
                        i = v.tensor_scalar(out=row(4), in0=row(3), scalar1=EPS,
                                            scalar2=None, op0=ALU.add)
                        sem.inc("vv", i)
                    veps_done = sem.val("vv")
                    bank_free[4] = ("vv", veps_done)
                    bank_free[5] = ("vv", veps_done)

                    @block.scalar
                    def _(sc):
                        sem.wait(sc, "scalar", "vv", veps_done)
                        i = sc.activation(out=row(5), in_=row(4), func=AF.Sqrt)
                        sem.inc("ev", i)
                    sq_done = sem.val("ev")

                    @block.vector
                    def _(v):
                        sem.wait(v, "vector", "ev", sq_done)
                        v.reciprocal(row(6), row(5))
                        v.tensor_mul(row(2), row(6), row(6))
                        v.tensor_mul(row(1), row(2), row(4))
                        v.tensor_scalar(out=row(2), in0=row(1), scalar1=-0.5,
                                        scalar2=1.5, op0=ALU.mult, op1=ALU.add)
                        v.tensor_mul(row(7), row(6), row(2))
                        v.tensor_mul(row(8), row(0), row(7))
                        i = v.tensor_scalar(out=row(8), in0=row(8), scalar1=-1.0,
                                            scalar2=None, op0=ALU.mult)
                        sem.inc("vv", i)
                    ab_done = sem.val("vv")

                    @block.sync
                    def _(sync, t0=t0):
                        sem.wait(sync, "sync", "vv", ab_done)
                        sync.dma_start(out=ab_d[0:1, t0:t0 + TT], in_=row(7)).then_inc(sem.h["sts"], 16)
                        sem.cnt["sts"] += 16
                        i = sync.dma_start(out=ab_d[1:2, t0:t0 + TT], in_=row(8))
                        sem.inc("sts", i, 16)
                    ab_sp = sem.val("sts")
                    ab_sp_prev = ab_sp

                    @block.gpsimd
                    def _(gp, t0=t0):
                        sem.wait(gp, "gpsimd", "sts", ab_sp)
                        if abc_free is not None:
                            sem.wait(gp, "gpsimd", *abc_free)
                        gp.dma_start(out=a_bc[:, :],
                                     in_=ab_d[0:1, t0:t0 + TT].broadcast_to([P, TT])).then_inc(sem.h["ldg"], 16)
                        sem.cnt["ldg"] += 16
                        i = gp.dma_start(out=b_bc[:, :],
                                         in_=ab_d[1:2, t0:t0 + TT].broadcast_to([P, TT]))
                        sem.inc("ldg", i, 16)
                    abc_ld2 = sem.val("ldg")

                    @block.vector
                    def _(v, b=b):
                        sem.wait(v, "vector", "ldg", abc_ld2)
                        if xn_free[b] is not None:
                            for cond in xn_free[b]:
                                sem.wait(v, "vector", *cond)
                        v.tensor_tensor(
                            out=xt[b][:, :].rearrange("p (a t) -> p a t", a=EH),
                            in0=xt[b][:, :].rearrange("p (a t) -> p a t", a=EH),
                            in1=a_bc[:, :].unsqueeze(1).broadcast_to([P, EH, TT]),
                            op=ALU.mult)
                        i = v.tensor_tensor(
                            out=xn_t[b][:, :].rearrange("p (a t) -> p a t", a=EH),
                            in0=xt[b][:, :].rearrange("p (a t) -> p a t", a=EH),
                            in1=b_bc[:, :].unsqueeze(1).broadcast_to([P, EH, TT]),
                            op=ALU.add)
                        sem.inc("vv", i)
                    xn_done_t[tt] = sem.val("vv")
                    xt_free[b] = ("vv", xn_done_t[tt])
                    abc_free = ("vv", xn_done_t[tt])

                    @block.sync
                    def _(sync, b=b, t0=t0, tt=tt):
                        sem.wait(sync, "sync", "vv", xn_done_t[tt])
                        i = sync.dma_start(out=xn3[:, :, t0:t0 + TT],
                                           in_=xn_t[b][:, :].rearrange("p (a t) -> p a t", a=EH))
                        sem.inc("sts", i, 16)
                    spill_cnt.append(sem.val("sts"))

                def stage_b(tt):
                    nonlocal bc_sp_prev
                    b = tt % 2
                    t0 = tt * TT
                    ev_base2 = sem.val("ev")
                    for jb in [8] + list(range(8)):
                        bank = 2 if jb == 8 else jb % 2
                        mwid = 32 if jb == 8 else P

                        @block.tensor
                        def _(te, b=b, tt=tt, jb=jb, bank=bank, mwid=mwid):
                            sem.wait(te, "tensor", "vv", xn_done_t[tt])
                            sem.wait(te, "tensor", "ldg", wxt_ld)
                            if jb in (8, 0, 1):
                                if bank_free[bank] is not None:
                                    sem.wait(te, "tensor", *bank_free[bank])
                            else:
                                sem.wait(te, "tensor", "ev", ev_base2 + jb)
                            for kc in range(EH):
                                i = te.matmul(
                                    out=ps[bank][0:mwid, 0:TT],
                                    lhsT=WxT[:, kc * J + jb * P:kc * J + jb * P + mwid],
                                    rhs=xn_t[b][:, kc * TT:(kc + 1) * TT],
                                    start=(kc == 0), stop=(kc == EH - 1))
                            sem.inc("mm", i)
                            dlb_evcnt[(tt, jb)] = sem.val("mm")
                    wx_done = sem.val("mm")
                    xn_free[b] = [("sts", spill_cnt[tt]), ("mm", wx_done)]

                    @block.scalar
                    def _(sc, b=b, tt=tt):
                        sem.wait(sc, "scalar", "mm", dlb_evcnt[(tt, 8)])
                        if bc_sp_prev is not None:
                            sem.wait(sc, "scalar", "sts", bc_sp_prev)
                        i = sc.activation(out=BCs[:, :], in_=ps[2][0:32, 0:TT],
                                          func=AF.Copy)
                        sem.inc("ev", i)
                        if dl_free[b] is not None:
                            sem.wait(sc, "scalar", *dl_free[b])
                        for jb in range(8):
                            bank = jb % 2
                            sem.wait(sc, "scalar", "mm", dlb_evcnt[(tt, jb)])
                            sc.activation(out=spsc[:, :], in_=ps[bank][:, 0:TT],
                                          func=AF.Exp)
                            i = sc.activation(out=dl_t[b][:, jb * TT:(jb + 1) * TT],
                                              in_=spsc[:, :], func=AF.Ln, bias=1.0)
                            sem.inc("ev", i)
                    assert sem.val("ev") == ev_base2 + 9
                    bank_free[2] = ("ev", ev_base2 + 1)
                    bank_free[0] = ("ev", ev_base2 + 8)
                    bank_free[1] = ("ev", ev_base2 + 9)
                    soft_done = sem.val("ev")

                    @block.sync
                    def _(sync, b=b, t0=t0, tt=tt):
                        sem.wait(sync, "sync", "ev", soft_done)
                        sync.dma_start(out=bc_d[:, t0:t0 + TT],
                                       in_=BCs[:, :]).then_inc(sem.h["sts"], 16)
                        sem.cnt["sts"] += 16
                        i = sync.dma_start(out=delta3[:, :, t0:t0 + TT],
                                           in_=dl_t[b][:, :].rearrange("p (a t) -> p a t", a=EH))
                        sem.inc("sts", i, 16)
                    dl_spill.append(sem.val("sts"))
                    bc_sp.append(sem.val("sts"))
                    bc_sp_prev = sem.val("sts")
                    dl_free[b] = ("sts", dl_spill[-1])

                xsq_free = [None]
                stage_a(0)
                stage_a(1)
                for tt in range(NTILE):
                    stage_b(tt)
                    if tt + 2 < NTILE:
                        stage_a(tt + 2)

            barrier()

            if phases < 3:
                @block.sync
                def _(sync):
                    sem.wait(sync, "sync", "sts", sem.val("sts"))
                return nc
            with ExitStack() as s3:
                xn_c = [sb(s3, f"xnc{b}", [P, EH * CH], BF16) for b in range(2)]
                dl_c = [sb(s3, f"dlc{b}", [P, EH * CH], BF16) for b in range(2)]
                dtx = sb(s3, "dtx", [P, EH * CH], BF16)
                dxn = sb(s3, "dxn", [P, EH * CH], BF16)
                dA = [sb(s3, f"dA{b}", [P, G * TS], BF16) for b in range(2)]
                ub = sb(s3, "ub", [P, G * TS], BF16)
                hb = sb(s3, "hb", [P, G * TS], BF16)
                l4f = sb(s3, "l4f", [P, EH * TS], F32)
                yb = [sb(s3, f"yb{b}", [P, EH * CH], BF16) for b in range(2)]
                att_t = sb(s3, "att_t", [P, EH * CH], F32)
                carry = sb(s3, "carry", [P, G], F32)
                dA0 = sb(s3, "dA0v", [P, G], F32)
                tmpg = sb(s3, "tmpg", [P, G], F32)
                Bb = [sb(s3, f"Bb{b}", [P, NST * TS], BF16) for b in range(2)]
                Cb = [sb(s3, f"Cb{b}", [P, NST * TS], BF16) for b in range(2)]

                @block.vector
                def _(v):
                    v.memset(carry[:, :], 0.0)

                @block.sync
                def _(sync):
                    sem.wait(sync, "sync", "sts", dl_spill[0])
                    sync.dma_start(out=xn_c[0][:, :].rearrange("p (a t) -> p a t", a=EH),
                                   in_=xn3[:, :, 0:CH]).then_inc(sem.h["lds"], 16)
                    sem.cnt["lds"] += 16
                    i = sync.dma_start(out=dl_c[0][:, :].rearrange("p (a t) -> p a t", a=EH),
                                       in_=delta3[:, :, 0:CH])
                    sem.inc("lds", i, 16)
                xn_ld = [sem.val("lds")]

                sub_done = []
                y_done = []
                wout_ev = []
                att_sp = []
                xnc_free = [None, None]
                y_free = [None, None]
                dlb_evcnt = {}

                def emit_wout(cc):
                    bcy = cc % 2
                    evw_base = sem.val("ev")
                    for eb in range(EH):
                        bank = 2 + eb % 2

                        @block.tensor
                        def _(te, cc=cc, bcy=bcy, eb=eb, bank=bank):
                            sem.wait(te, "tensor", "vv", y_done[cc])
                            sem.wait(te, "tensor", "ldg", wx_ld)
                            if eb < 2:
                                if bank_free[bank] is not None:
                                    sem.wait(te, "tensor", *bank_free[bank])
                            else:
                                sem.wait(te, "tensor", "ev", evw_base + eb - 1)
                            for kc in range(EH):
                                i = te.matmul(
                                    out=ps[bank][:, 0:CH],
                                    lhsT=Wot[:, kc * H + eb * P:kc * H + (eb + 1) * P],
                                    rhs=yb[bcy][:, kc * CH:(kc + 1) * CH],
                                    start=(kc == 0), stop=(kc == EH - 1))
                            sem.inc("mm", i)
                            dlb_evcnt[("wo", cc, eb)] = sem.val("mm")

                    @block.scalar
                    def _(sc, cc=cc):
                        if att_sp:
                            sem.wait(sc, "scalar", "sts", att_sp[-1])
                        for eb in range(EH):
                            bank = 2 + eb % 2
                            sem.wait(sc, "scalar", "mm", dlb_evcnt[("wo", cc, eb)])
                            i = sc.activation(out=att_t[:, eb * CH:(eb + 1) * CH],
                                              in_=ps[bank][:, 0:CH], func=AF.Copy)
                            sem.inc("ev", i)
                    assert sem.val("ev") == evw_base + EH
                    bank_free[2] = ("ev", evw_base + EH - 1)
                    bank_free[3] = ("ev", evw_base + EH)
                    wout_ev.append(sem.val("ev"))

                    @block.sync
                    def _(sync, cc=cc):
                        sem.wait(sync, "sync", "ev", wout_ev[-1])
                        i = sync.dma_start(out=attn3[:, :, cc * CH:(cc + 1) * CH],
                                           in_=att_t[:, :].rearrange("p (a t) -> p a t", a=EH))
                        sem.inc("sts", i, 16)
                    att_sp.append(sem.val("sts"))
                    y_free[cc % 2] = ("mm", dlb_evcnt[("wo", cc, EH - 1)])

                for cc in range(NCHUNK):
                    bcc = cc % 2
                    ct0 = cc * CH

                    if cc + 1 < NCHUNK:
                        @block.sync
                        def _(sync, cc=cc):
                            nb = (cc + 1) % 2
                            sem.wait(sync, "sync", "sts", dl_spill[cc + 1])
                            if xnc_free[nb] is not None:
                                sem.wait(sync, "sync", *xnc_free[nb])
                            sync.dma_start(
                                out=xn_c[nb][:, :].rearrange("p (a t) -> p a t", a=EH),
                                in_=xn3[:, :, (cc + 1) * CH:(cc + 2) * CH]).then_inc(sem.h["lds"], 16)
                            sem.cnt["lds"] += 16
                            i = sync.dma_start(
                                out=dl_c[nb][:, :].rearrange("p (a t) -> p a t", a=EH),
                                in_=delta3[:, :, (cc + 1) * CH:(cc + 2) * CH])
                            sem.inc("lds", i, 16)
                        xn_ld.append(sem.val("lds"))

                    @block.vector
                    def _(v, cc=cc, bcc=bcc):
                        sem.wait(v, "vector", "lds", xn_ld[cc])
                        v.tensor_tensor(out=dtx[:, :], in0=dl_c[bcc][:, :],
                                        in1=xn_c[bcc][:, :], op=ALU.mult)
                        for chx in range(EH):
                            i = v.tensor_scalar(out=dxn[:, chx * CH:(chx + 1) * CH],
                                                in0=xn_c[bcc][:, chx * CH:(chx + 1) * CH],
                                                scalar1=D_sb[:, chx:chx + 1],
                                                scalar2=None, op0=ALU.mult)
                        sem.inc("vv", i)
                    dtx_done = sem.val("vv")

                    for s in range(NSUB):
                        gs = cc * NSUB + s
                        bs_ = s % 2
                        ts0 = s * TS
                        gt0 = ct0 + ts0

                        @block.gpsimd
                        def _(gp, gs=gs, bs_=bs_, gt0=gt0, cc=cc):
                            sem.wait(gp, "gpsimd", "sts", bc_sp[cc])
                            if gs >= 2:
                                sem.wait(gp, "gpsimd", "vv", sub_done[gs - 2])
                            gp.dma_start(
                                out=Bb[bs_][:, :].rearrange("p (n t) -> p n t", n=NST),
                                in_=bc_d[0:NST, gt0:gt0 + TS].unsqueeze(0).broadcast_to([P, NST, TS])
                            ).then_inc(sem.h["ldg"], 16)
                            sem.cnt["ldg"] += 16
                            i = gp.dma_start(
                                out=Cb[bs_][:, :].rearrange("p (n t) -> p n t", n=NST),
                                in_=bc_d[NST:32, gt0:gt0 + TS].unsqueeze(0).broadcast_to([P, NST, TS]))
                            sem.inc("ldg", i, 16)
                        bc_ld = sem.val("ldg")

                        @block.scalar
                        def _(sc, gs=gs, bs_=bs_, ts0=ts0, bcc=bcc, cc=cc):
                            sem.wait(sc, "scalar", "lds", xn_ld[cc])
                            if gs >= 2:
                                sem.wait(sc, "scalar", "vv", sub_done[gs - 2])
                            d3 = dl_c[bcc][:, :].rearrange("p (a t) -> p a t", a=EH)[:, :, ts0:ts0 + TS]
                            for n in range(NST):
                                i = sc.activation(
                                    out=dA[bs_][:, n * EH * TS:(n + 1) * EH * TS]
                                        .rearrange("p (a t) -> p a t", a=EH),
                                    in_=d3, func=AF.Exp, scale=A_bc[:, n:n + 1])
                            sem.inc("ev", i)
                        exp_done = sem.val("ev")

                        yfree = y_free[cc % 2] if s == 0 else None

                        @block.vector
                        def _(v, gs=gs, bs_=bs_, ts0=ts0, yfree=yfree):
                            sem.wait(v, "vector", "ldg", bc_ld)
                            sem.wait(v, "vector", "ev", exp_done)
                            if yfree is not None:
                                sem.wait(v, "vector", *yfree)
                            d4 = dtx[:, :].rearrange("p (a t) -> p a t", a=EH)[:, :, ts0:ts0 + TS]
                            u4 = ub[:, :].rearrange("p (n a t) -> p n a t", n=NST, a=EH)
                            v.tensor_tensor(
                                out=u4,
                                in0=d4.unsqueeze(1).broadcast_to([P, NST, EH, TS]),
                                in1=Bb[bs_][:, :].rearrange("p (n t) -> p n t", n=NST)
                                    .unsqueeze(2).broadcast_to([P, NST, EH, TS]),
                                op=ALU.mult)
                            dA3 = dA[bs_][:, :].rearrange("p (g t) -> p g t", g=G)
                            v.tensor_copy(out=dA0[:, :].unsqueeze(2), in_=dA3[:, :, 0:1])
                            v.tensor_mul(tmpg[:, :], dA0[:, :], carry[:, :])
                            u3 = ub[:, :].rearrange("p (g t) -> p g t", g=G)
                            v.tensor_tensor(out=u3[:, :, 0:1], in0=u3[:, :, 0:1],
                                            in1=tmpg[:, :].unsqueeze(2), op=ALU.add)
                            v.memset(dA3[:, :, 0:1], 0.0)
                            v.tensor_tensor_scan(out=hb[:, :], data0=dA[bs_][:, :],
                                                 data1=ub[:, :], initial=0.0,
                                                 op0=ALU.mult, op1=ALU.add)
                            h3 = hb[:, :].rearrange("p (g t) -> p g t", g=G)
                            v.tensor_copy(out=carry[:, :].unsqueeze(2), in_=h3[:, :, TS - 1:TS])
                            v.tensor_tensor(
                                out=u4, in0=hb[:, :].rearrange("p (n a t) -> p n a t", n=NST, a=EH),
                                in1=Cb[bs_][:, :].rearrange("p (n t) -> p n t", n=NST)
                                    .unsqueeze(2).broadcast_to([P, NST, EH, TS]),
                                op=ALU.mult)
                            half = G * TS // 2
                            v.tensor_tensor(out=hb[:, 0:half], in0=ub[:, 0:half],
                                            in1=ub[:, half:2 * half], op=ALU.add)
                            q = half // 2
                            v.tensor_tensor(out=hb[:, half:half + q], in0=hb[:, 0:q],
                                            in1=hb[:, q:2 * q], op=ALU.add)
                            e8 = q // 2
                            v.tensor_tensor(out=hb[:, half + q:half + q + e8],
                                            in0=hb[:, half:half + e8],
                                            in1=hb[:, half + e8:half + 2 * e8], op=ALU.add)
                            e16 = e8 // 2
                            v.tensor_tensor(out=l4f[:, :], in0=hb[:, half + q:half + q + e16],
                                            in1=hb[:, half + q + e16:half + q + 2 * e16], op=ALU.add)
                            i = v.tensor_tensor(
                                out=yb[gs // NSUB % 2][:, :].rearrange("p (a t) -> p a t", a=EH)[:, :, ts0:ts0 + TS],
                                in0=l4f[:, :].rearrange("p (a t) -> p a t", a=EH),
                                in1=dxn[:, :].rearrange("p (a t) -> p a t", a=EH)[:, :, ts0:ts0 + TS],
                                op=ALU.add)
                            sem.inc("vv", i)
                        sub_done.append(sem.val("vv"))

                    y_done.append(sub_done[-1])
                    xnc_free[bcc] = ("vv", sub_done[-1])
                    if cc + 2 < NCHUNK:
                        pass
                    emit_wout(cc)

                att_sp_outer = list(att_sp)
            barrier()

        barrier()

        if phases < 4:
            @block.sync
            def _(sync):
                sem.wait(sync, "sync", "sts", sem.val("sts"))
            return nc
        att_sp_l = att_sp_outer
        with ExitStack() as s4:
            w1h = sb(s4, "w1h", [P, EH * (FH // 2)], BF16)
            w2h = sb(s4, "w2h", [P, (FH // 256) * H], BF16)
            xn_t4 = [sb(s4, f"xn4{b}", [P, EH * TT], BF16) for b in range(2)]
            res_t = [sb(s4, f"res{b}", [P, EH * TT], F32) for b in range(2)]
            att_t4 = [sb(s4, "at4", [P, EH * TT], F32)] * 2
            h_bf = sb(s4, "h_bf", [P, 16 * TT], BF16)
            fB = sb(s4, "fB", [P, EH * TT], F32)
            out_t = [sb(s4, "out1", [P, EH * TT], F32)] * 2

            HHALF = FH // 2

            for pas in range(2):
                hoff = pas * HHALF

                @block.gpsimd
                def _(gp, pas=pas, hoff=hoff):
                    if pas == 1:
                        sem.wait(gp, "gpsimd", "mm", w_used[0])
                    gp.dma_start(
                        out=w1h[:, :].rearrange("p (a h) -> p a h", a=EH),
                        in_=w1.rearrange("(a p) h -> p a h", p=P)[:, :, hoff:hoff + HHALF]
                    ).then_inc(sem.h["ldg"], 16)
                    sem.cnt["ldg"] += 16
                    i = gp.dma_start(
                        out=w2h[:, :].rearrange("p (a e) -> p a e", a=16),
                        in_=w2.rearrange("(a p) e -> p a e", p=P)[:, pas * 16:(pas + 1) * 16, :])
                    sem.inc("ldg", i, 16)
                w_ld = sem.val("ldg")
                w_used = [None]
                if pas == 0:
                    acca_sp = []

                gelu_cnt = {}
                ffn2_cnt = {}
                res_free = [None, None]
                xn4_free = [None, None]
                att4_free = [None, None]
                out_free = [None, None]
                h_free = None
                fin_prev = None

                for tt in range(NTILE):
                    b = tt % 2
                    t0 = tt * TT

                    @block.sync
                    def _(sync, pas=pas, b=b, t0=t0):
                        if xn4_free[b] is not None:
                            sem.wait(sync, "sync", *xn4_free[b])
                        sync.dma_start(out=xn_t4[b][:, :].rearrange("p (a t) -> p a t", a=EH),
                                       in_=xn3[:, :, t0:t0 + TT]).then_inc(sem.h["lds"], 16)
                        sem.cnt["lds"] += 16
                        if res_free[b] is not None:
                            sem.wait(sync, "sync", *res_free[b])
                        if pas == 1:
                            sem.wait(sync, "sync", "sts", acca_sp[tt])
                        src = xin3 if pas == 0 else acca3
                        i = sync.dma_start(out=res_t[b][:, :].rearrange("p (a t) -> p a t", a=EH),
                                           in_=src[:, :, t0:t0 + TT])
                        sem.inc("lds", i, 16)
                        if pas == 1:
                            if att4_free[0] is not None:
                                sem.wait(sync, "sync", *att4_free[0])
                            sem.wait(sync, "sync", "sts", att_sp_l[tt])
                            i = sync.dma_start(out=att_t4[b][:, :].rearrange("p (a t) -> p a t", a=EH),
                                               in_=attn3[:, :, t0:t0 + TT])
                            sem.inc("lds", i, 16)
                    t_ld = sem.val("lds")

                    evt_base = sem.val("ev")
                    for hb_ in range(16):
                        bank = hb_ % 2

                        @block.tensor
                        def _(te, pas=pas, b=b, tt=tt, hb_=hb_, bank=bank):
                            sem.wait(te, "tensor", "lds", t_ld)
                            sem.wait(te, "tensor", "ldg", w_ld)
                            if hb_ == 0 and h_free is not None:
                                sem.wait(te, "tensor", *h_free)
                            if hb_ < 2:
                                if bank_free[bank] is not None:
                                    sem.wait(te, "tensor", *bank_free[bank])
                            else:
                                sem.wait(te, "tensor", "ev", evt_base + hb_ - 1)
                            for kc in range(EH):
                                i = te.matmul(
                                    out=ps[bank][:, 0:TT],
                                    lhsT=w1h[:, kc * HHALF + hb_ * P:kc * HHALF + (hb_ + 1) * P],
                                    rhs=xn_t4[b][:, kc * TT:(kc + 1) * TT],
                                    start=(kc == 0), stop=(kc == EH - 1))
                            sem.inc("mm", i)
                            gelu_cnt[(pas, tt, hb_)] = sem.val("mm")

                    @block.scalar
                    def _(sc, pas=pas, tt=tt):
                        for hb_ in range(16):
                            bank = hb_ % 2
                            sem.wait(sc, "scalar", "mm", gelu_cnt[(pas, tt, hb_)])
                            i = sc.activation(out=h_bf[:, hb_ * TT:(hb_ + 1) * TT],
                                              in_=ps[bank][:, 0:TT], func=AF.Gelu,
                                              bias=B1_sb[:, pas * 16 + hb_:pas * 16 + hb_ + 1])
                            sem.inc("ev", i)
                    assert sem.val("ev") == evt_base + 16
                    bank_free[0] = ("ev", evt_base + 15)
                    bank_free[1] = ("ev", evt_base + 16)
                    gelu_done = sem.val("ev")

                    evq_base = sem.val("vv") if pas == 0 else sem.val("ev")
                    for eb in range(EH):
                        bank = 2 + eb % 2

                        @block.tensor
                        def _(te, pas=pas, tt=tt, eb=eb, bank=bank):
                            sem.wait(te, "tensor", "ev", gelu_done)
                            if eb < 2:
                                if bank_free[bank] is not None:
                                    sem.wait(te, "tensor", *bank_free[bank])
                            else:
                                sem.wait(te, "tensor", "vv" if pas == 0 else "ev",
                                         evq_base + eb - 1)
                            for kc in range(16):
                                i = te.matmul(
                                    out=ps[bank][:, 0:TT],
                                    lhsT=w2h[:, kc * H + eb * P:kc * H + (eb + 1) * P],
                                    rhs=h_bf[:, kc * TT:(kc + 1) * TT],
                                    start=(kc == 0), stop=(kc == 15))
                            sem.inc("mm", i)
                            ffn2_cnt[(pas, tt, eb)] = sem.val("mm")

                    if pas == 0:
                        @block.vector
                        def _(v, tt=tt, b=b):
                            if out_free[0] is not None:
                                sem.wait(v, "vector", *out_free[0])
                            for eb in range(EH):
                                bank = 2 + eb % 2
                                sem.wait(v, "vector", "mm", ffn2_cnt[(0, tt, eb)])
                                i = v.tensor_tensor(out=out_t[b][:, eb * TT:(eb + 1) * TT],
                                                    in0=ps[bank][:, 0:TT],
                                                    in1=res_t[b][:, eb * TT:(eb + 1) * TT],
                                                    op=ALU.add)
                                sem.inc("vv", i)
                        assert sem.val("vv") == evq_base + EH
                        bank_free[2] = ("vv", evq_base + EH - 1)
                        bank_free[3] = ("vv", evq_base + EH)
                    else:
                        @block.scalar
                        def _(sc, tt=tt):
                            if fin_prev is not None:
                                sem.wait(sc, "scalar", "vv", fin_prev)
                            for eb in range(EH):
                                bank = 2 + eb % 2
                                sem.wait(sc, "scalar", "mm", ffn2_cnt[(1, tt, eb)])
                                i = sc.activation(out=fB[:, eb * TT:(eb + 1) * TT],
                                                  in_=ps[bank][:, 0:TT], func=AF.Identity,
                                                  bias=B2_sb[:, eb:eb + 1])
                                sem.inc("ev", i)
                        assert sem.val("ev") == evq_base + EH
                        bank_free[2] = ("ev", evq_base + EH - 1)
                        bank_free[3] = ("ev", evq_base + EH)

                    h_free = ("mm", ffn2_cnt[(pas, tt, EH - 1)])
                    if tt == NTILE - 1:
                        w_used[0] = ffn2_cnt[(pas, tt, EH - 1)]

                    if pas == 0:
                        acc_done = sem.val("vv")
                        res_free[b] = ("vv", acc_done)
                        xn4_free[b] = h_free

                        @block.sync
                        def _(sync, tt=tt, b=b, t0=t0):
                            sem.wait(sync, "sync", "vv", acc_done)
                            i = sync.dma_start(out=acca3[:, :, t0:t0 + TT],
                                               in_=out_t[b][:, :].rearrange("p (a t) -> p a t", a=EH))
                            sem.inc("sts", i, 16)
                        acca_sp.append(sem.val("sts"))
                        out_free[0] = ("sts", sem.val("sts"))
                    else:
                        fB_done = sem.val("ev")

                        @block.vector
                        def _(v, tt=tt, b=b):
                            sem.wait(v, "vector", "ev", fB_done)
                            if out_free[0] is not None:
                                sem.wait(v, "vector", *out_free[0])
                            v.tensor_tensor(out=out_t[b][:, :], in0=fB[:, :],
                                            in1=res_t[b][:, :], op=ALU.add)
                            i = v.tensor_tensor(out=out_t[b][:, :], in0=out_t[b][:, :],
                                                in1=att_t4[b][:, :], op=ALU.add)
                            sem.inc("vv", i)
                        fin_done = sem.val("vv")
                        fin_prev = fin_done
                        res_free[b] = ("vv", fin_done)
                        att4_free[0] = ("vv", fin_done)
                        xn4_free[b] = h_free

                        @block.sync
                        def _(sync, tt=tt, b=b, t0=t0):
                            sem.wait(sync, "sync", "vv", fin_done)
                            i = sync.dma_start(out=yout3[:, :, t0:t0 + TT],
                                               in_=out_t[b][:, :].rearrange("p (a t) -> p a t", a=EH))
                            sem.inc("sts", i, 16)
                        out_free[0] = ("sts", sem.val("sts"))

            @block.sync
            def _(sync):
                sem.wait(sync, "sync", "sts", sem.val("sts"))

    return nc


def kernel(L0_attn_x, L0_attn_Wx, L0_attn_A_log, L0_attn_D, L0_attn_Wout,
           L0_ffn_w1, L0_ffn_b1, L0_ffn_w2, L0_ffn_b2):
    x = np.asarray(L0_attn_x, dtype=np.float32)
    Wx = np.asarray(L0_attn_Wx, dtype=np.float32)
    A_log = np.asarray(L0_attn_A_log, dtype=np.float32)
    D = np.asarray(L0_attn_D, dtype=np.float32)
    Wout = np.asarray(L0_attn_Wout, dtype=np.float32)
    w1 = np.asarray(L0_ffn_w1, dtype=np.float32)
    b1 = np.asarray(L0_ffn_b1, dtype=np.float32)
    w2 = np.asarray(L0_ffn_w2, dtype=np.float32)
    b2 = np.asarray(L0_ffn_b2, dtype=np.float32)

    wxt = np.ascontiguousarray(np.concatenate([Wx[32:], Wx[:32]], axis=0).T)
    wot = np.ascontiguousarray(Wout.T)
    w1c = np.ascontiguousarray(w1)
    w2c = np.ascontiguousarray(w2)
    alog = np.ascontiguousarray(A_log.reshape(1, 16))
    dvec = np.ascontiguousarray(D.reshape(8, 128).T)
    b1m = np.ascontiguousarray(b1.reshape(32, 128).T)
    b2m = np.ascontiguousarray(b2.reshape(8, 128).T)

    nc = build_program()
    in_maps = []
    for i in range(8):
        in_maps.append({
            "xin": np.ascontiguousarray(x[i].T),
            "wxt": wxt, "w1": w1c, "w2": w2c, "wot": wot,
            "alog": alog, "dvec": dvec, "b1m": b1m, "b2m": b2m,
        })
    res = run_bass_kernel_spmd(nc, in_maps, core_ids=list(range(8)))
    out = np.stack([np.asarray(r["yout"], dtype=np.float32).T for r in res.results], axis=0)
    return np.ascontiguousarray(out)
